# version 01 (final)
# baseline (speedup 1.0000x reference)
import math
from contextlib import ExitStack
import numpy as np
import concourse.bass as bass
import concourse.mybir as mybir
from concourse.bass_utils import run_bass_kernel_spmd

F32 = mybir.dt.float32
BF16 = mybir.dt.bfloat16
FP16 = mybir.dt.float16
U8 = mybir.dt.uint8
AF = mybir.ActivationFunctionType
ALU = mybir.AluOpType
AX = mybir.AxisListType

T = 2048
D = 1024
DC = 8
NTT = 4
TTW = 512
H = 16
HD = 64
QT = 16
KC = 16
KTOP = 1433
Z70 = -0.524401
THRESH = 0.02
NE = 32
EI = 512
LOCAL_HALF = 16

ENGS = ("sync", "tensor", "vector", "scalar", "gpsimd")


class _Counts:

    def __init__(self, ctx):
        self._ctx = ctx
        self._d = {}

    def __getitem__(self, k):
        self._ctx.flush(k)
        return self._d[k]

    def raw(self, k):
        return self._d[k]

    def bump(self, k, n):
        self._d[k] += n

    def keys(self):
        return self._d.keys()


class Ctx:

    def __init__(self, nc, block, stack, dbg=False):
        self.nc = nc
        self.block = block
        self.stack = stack
        self.dbg = dbg
        self.dbg_outs = []
        self.c = _Counts(self)
        self.pending = {}
        self.sems = {}
        for name in ("pe", "dve", "act", "gps", "wdma", "xdma", "st",
                     "wda", "wdb", "cbs", "xda", "xdb", "sta", "stb"):
            self.sems[name] = stack.enter_context(nc.semaphore(name))
            self.c._d[name] = 0

    def sb(self, name, shape, dtype, stack=None):
        return (stack or self.stack).enter_context(self.nc.sbuf_tensor(name, shape, dtype))

    def ps(self, name, shape, dtype=F32):
        return self.stack.enter_context(self.nc.psum_tensor(name, shape, dtype))

    def flush(self, sem):
        pend = self.pending.pop(sem, None)
        if pend is not None:
            pend.then_inc(self.sems[sem], 1)
            self.c.bump(sem, 1)

    def defer_inc(self, inst, sem):
        self.flush(sem)
        self.pending[sem] = inst

    def inc(self, inst, sem, n=1):
        if self.pending.get(sem) is inst:
            self.pending.pop(sem)
        else:
            self.flush(sem)
        inst.then_inc(self.sems[sem], n)
        self.c.bump(sem, n)
        return self.c.raw(sem)

    def snap(self):
        return {k: self.c[k] for k in self.c.keys()}

    def wait(self, e, a, b=None):
        if b is None:
            for k, v in a.items():
                if v > 0:
                    e.wait_ge(self.sems[k], v)
        else:
            if b > 0:
                e.wait_ge(self.sems[a], b)

    def fence(self, e, inst, sem="dve"):
        n = self.inc(inst, sem)
        e.wait_ge(self.sems[sem], n)
        return n

    def barrier(self, engines=ENGS):
        snap = self.snap()
        for eng in engines:
            @getattr(self.block, eng)
            def _(e, snap=snap):
                self.wait(e, snap)

    def tap(self, name, src_ap, shape, dtype=F32):
        if not self.dbg:
            return
        d = self.nc.dram_tensor("dbg_" + name, shape, dtype, kind="ExternalOutput").ap()
        self.dbg_outs.append(name)
        snap = self.snap()
        @self.block.sync
        def _(e):
            self.wait(e, snap)
            n = self.inc(e.dma_start(out=d, in_=src_ap), "st", 16)
            e.wait_ge(self.sems["st"], n)



_FENCE_OPS = {
    "tensor_tensor", "tensor_scalar", "tensor_copy", "tensor_add", "tensor_sub",
    "tensor_mul", "tensor_max", "tensor_relu", "tensor_single_scalar",
    "scalar_tensor_tensor", "tensor_reduce", "reduce_sum", "reduce_max",
    "memset", "memzero", "reciprocal", "select", "copy_predicated",
    "tensor_scalar_add", "tensor_scalar_sub", "tensor_scalar_mul",
    "tensor_scalar_max", "tensor_scalar_min", "tensor_tensor_scan",
    "activation", "copy", "add", "mul", "sign", "sqrt", "square",
    "affine_select", "iota",
}


class _FencedEngine:

    def __init__(self, ctx, e, sem):
        self._ctx, self._e, self._sem = ctx, e, sem

    def __getattr__(self, name):
        f = getattr(self._e, name)
        if self._sem is not None and name in _FENCE_OPS and callable(f):
            ctx, e, sem = self._ctx, self._e, self._sem
            def wrapped(*a, **k):
                ctx.wait(e, sem, ctx.c[sem])
                inst = f(*a, **k)
                ctx.defer_inc(inst, sem)
                return inst
            return wrapped
        return f


class FencingBlock:
    _SEM = {"vector": "dve", "scalar": "act", "gpsimd": "gps",
            "tensor": None, "sync": None}

    def __init__(self, ctx, block):
        self._ctx, self._block = ctx, block

    def _mk(self, eng):
        dec = getattr(self._block, eng)
        sem = self._SEM[eng]
        ctx = self._ctx
        def deco(fn):
            def shim(e, *a, **k):
                return fn(_FencedEngine(ctx, e, sem), *a, **k)
            return dec(shim)
        return deco

    @property
    def vector(self):
        return self._mk("vector")

    @property
    def scalar(self):
        return self._mk("scalar")

    @property
    def gpsimd(self):
        return self._mk("gpsimd")

    @property
    def tensor(self):
        return self._mk("tensor")

    @property
    def sync(self):
        return self._mk("sync")



def host_prepare(inputs):
    A = {}
    def fm(w):
        return np.ascontiguousarray(np.asarray(w).T)
    A["qkv_w_fm"] = fm(inputs["L0_attn_qkv_w"])
    A["proj_w_fm"] = fm(inputs["L0_attn_proj_w"])
    A["proj_b"] = np.ascontiguousarray(np.asarray(inputs["L0_attn_proj_b"]))
    A["feat_w_fm"] = fm(inputs["L0_attn_feat_w"])
    A["feat_b"] = np.ascontiguousarray(np.asarray(inputs["L0_attn_feat_b"]))
    A["ps1_w_fm"] = fm(inputs["L0_attn_ps1_w"])
    A["ps1_b"] = np.ascontiguousarray(np.asarray(inputs["L0_attn_ps1_b"]))
    A["ps2_w_fm"] = fm(inputs["L0_attn_ps2_w"])
    A["ps2_b"] = np.ascontiguousarray(np.asarray(inputs["L0_attn_ps2_b"]))
    A["ps3_w_fm"] = fm(inputs["L0_attn_ps3_w"])
    A["ps3_b"] = np.ascontiguousarray(np.asarray(inputs["L0_attn_ps3_b"]))
    A["pattern_bias"] = np.ascontiguousarray(np.asarray(inputs["L0_attn_pattern_bias"]))
    A["sparse_w"] = np.ascontiguousarray(np.asarray(inputs["L0_attn_sparse_w"]))
    A["sparse_b"] = np.ascontiguousarray(np.asarray(inputs["L0_attn_sparse_b"]))
    A["gate_w_fm"] = fm(inputs["L0_ffn_gate_weight"])
    A["e_bias"] = np.ascontiguousarray(np.asarray(inputs["L0_ffn_e_score_correction_bias"]))
    A["gu_fm"] = np.ascontiguousarray(np.transpose(np.asarray(inputs["L0_ffn_experts_gate_up"]), (0, 2, 1)))
    A["dn_fm"] = np.ascontiguousarray(np.transpose(np.asarray(inputs["L0_ffn_experts_down"]), (0, 2, 1)))
    A["shg_fm"] = fm(inputs["L0_ffn_shared_gate_w"])
    A["shu_fm"] = fm(inputs["L0_ffn_shared_up_w"])
    A["shd_fm"] = fm(inputs["L0_ffn_shared_down_w"])
    A["wdq_fm"] = fm(inputs["L1_attn_W_dq"])
    A["wuq_fm"] = fm(inputs["L1_attn_W_uq"])
    A["wuq_raw"] = np.ascontiguousarray(np.asarray(inputs["L1_attn_W_uq"]))
    A["wdq_raw"] = np.ascontiguousarray(np.asarray(inputs["L1_attn_W_dq"]))
    A["wdkv_fm"] = fm(inputs["L1_attn_W_dkv"])
    A["wuk_raw"] = np.ascontiguousarray(np.asarray(inputs["L1_attn_W_uk"]))
    A["wuv_raw"] = np.ascontiguousarray(np.asarray(inputs["L1_attn_W_uv"]))
    A["wo_fm"] = fm(inputs["L1_attn_W_o"])
    A["k0_u8"] = np.ascontiguousarray(np.asarray(inputs["L1_ffn_k0"]).view(np.uint8))
    A["k1_u8"] = np.ascontiguousarray(np.asarray(inputs["L1_ffn_k1"]).view(np.uint8))
    A["k2_u8"] = np.ascontiguousarray(np.asarray(inputs["L1_ffn_k2"]).view(np.uint8))
    A["proj_b_cols"] = np.ascontiguousarray(np.asarray(inputs["L0_attn_proj_b"]).reshape(8, 128).T)
    A["feat_b_cols"] = np.ascontiguousarray(np.asarray(inputs["L0_attn_feat_b"]).reshape(8, 128).T)
    A["ps1_b_cols"] = np.ascontiguousarray(np.asarray(inputs["L0_attn_ps1_b"]).reshape(8, 128).T)
    A["ps2_b_cols"] = np.ascontiguousarray(np.asarray(inputs["L0_attn_ps2_b"]).reshape(4, 128).T)
    A["s012"] = np.ascontiguousarray(np.stack([np.asarray(inputs["L1_ffn_s0"]),
                                               np.asarray(inputs["L1_ffn_s1"]),
                                               np.asarray(inputs["L1_ffn_s2"])]).reshape(3).astype(np.float32))
    return A


B0_INPUT_KEYS = ["qkv_w_fm", "proj_w_fm", "proj_b", "feat_w_fm", "feat_b",
                 "ps1_w_fm", "ps1_b", "ps2_w_fm", "ps2_b", "ps3_w_fm", "ps3_b",
                 "pattern_bias", "sparse_w", "sparse_b", "gate_w_fm", "e_bias",
                 "gu_fm", "dn_fm", "shg_fm", "shu_fm", "shd_fm"]
B1_INPUT_KEYS = ["wdq_fm", "wuq_fm", "wuq_raw", "wdq_raw", "wdkv_fm", "wuk_raw",
                 "wuv_raw", "wo_fm", "k0_u8", "k1_u8", "k2_u8", "s012"]




def emit_block0(ctx, io, upto=99):
    nc, blk = ctx.nc, FencingBlock(ctx, ctx.block)
    sb, ps = ctx.sb, ctx.ps

    A_xn = sb("A_xn", [128, DC * T], BF16)
    A_lo = sb("A_lo", [128, DC * T], BF16)
    A_acc = sb("A_acc", [128, DC * T], BF16)
    ones_col = sb("ones_col", [128, 1], F32)
    ones_colh = sb("ones_colh", [128, 1], FP16)
    ones_colb = sb("ones_colb", [128, 1], BF16)
    ones_row = sb("ones_row", [1, 128], F32)
    ident32 = sb("ident32", [128, 128], F32)
    identb = sb("identb", [128, 128], BF16)
    bandcf = sb("bandcf", [128, 128], FP16)
    bandmf = sb("bandmf", [128, 128], FP16)
    bandpf = sb("bandpf", [128, 128], FP16)
    band32 = [sb(f"band32_{i}", [128, 128], F32) for i in range(3)]
    pw_row = sb("pw_row", [1, 3], F32)
    rows64 = sb("rows64", [1, 64], F32)
    rows64m = sb("rows64m", [1, 64], F32)
    col_uv = sb("col_uv", [128, 64], F32)
    col_ms = sb("col_ms", [128, 64], F32)
    state = sb("state", [128, 24], F32)
    G_sb = sb("G_sb", [128, 8 * 64], BF16)
    kbar_cols = sb("kbar_cols", [128, 8], BF16)
    pooledS = sb("pooledS", [128, 8], F32)
    sumsqS = sb("sumsqS", [128, 8], F32)
    negmh = sb("negmh", [128, 1], F32)
    mh_scr = sb("mh_scr", [1, 1], F32)
    tau_row = sb("tau_row", [1, 128], F32)
    zrow = sb("zrow", [1, 128], F32)
    srow = sb("srow", [1, H], F32)
    enh_f = sb("enh_f", [128, 64], F32)
    enh_hi = sb("enh_hi", [128, 64], BF16)
    enh_lo = sb("enh_lo", [128, 64], BF16)
    svar_dram = nc.dram_tensor("svar_scr", [1], F32, kind="Internal").ap()
    lg_dram = nc.dram_tensor("lg_scr", [3], F32, kind="Internal").ap()
    ps_big = ps("ps_big", [128, 2048], F32)
    ps_a = ps("ps_a", [128, 512], F32)
    ps_b = ps("ps_b", [128, 512], F32)
    ps_c = ps("ps_c", [128, 512], F32)
    ps_d = ps("ps_d", [128, 512], F32)

    @blk.sync
    def _(e):
        ctx.inc(e.dma_start(out=srow[0:1, :], in_=io["spw"].unsqueeze(0)), "wdma", 16)

    @blk.vector
    def _(e):
        e.memset(ones_col[:], 1.0)
        e.memset(ones_colh[:], 1.0)
        e.memset(ones_colb[:], 1.0)
        e.memset(ones_row[:], 1.0)
        ctx.inc(e.memset(state[:], 0.0), "dve")

    @blk.gpsimd
    def _(e):
        for i, (lo, hi_) in enumerate(((-LOCAL_HALF, LOCAL_HALF), (112, 1000), (-1000, -112))):
            tf = band32[i]
            ctx.fence(e, e.memset(tf[:], 1.0), "gps")
            ctx.fence(e, e.affine_select(out=tf[:], in_=tf[:], compare_op=ALU.is_ge,
                            fill=0.0, base=hi_, pattern=[[1, 128]], channel_multiplier=-1), "gps")
            ctx.fence(e, e.affine_select(out=tf[:], in_=tf[:], compare_op=ALU.is_ge,
                            fill=0.0, base=-lo, pattern=[[-1, 128]], channel_multiplier=1), "gps")
        for idt in (ident32, identb):
            ctx.fence(e, e.memset(idt[:], 0.0), "gps")
            ctx.fence(e, e.affine_select(out=idt[:], in_=idt[:], compare_op=ALU.not_equal,
                            fill=1.0, base=0, pattern=[[-1, 128]], channel_multiplier=1), "gps")
        ctx.inc(e.engine_nop(), "gps")

    @blk.vector
    def _(e):
        ctx.wait(e, "gps", ctx.c["gps"])
        e.tensor_copy(bandcf[:], band32[0][:])
        e.tensor_copy(bandmf[:], band32[1][:])
        ctx.inc(e.tensor_copy(bandpf[:], band32[2][:]), "dve")
    ctx.barrier()

    with ExitStack() as ph:
        xch = [ctx.sb(f"xch{i}", [128, T], F32, ph) for i in range(2)]
        sqch = [ctx.sb(f"sqch{i}", [128, T], F32, ph) for i in range(2)]
        xnf = [ctx.sb(f"xnf{i}", [128, TTW], F32, ph) for i in range(2)]
        sq2 = [ctx.sb(f"sq2{i}", [128, TTW], F32, ph) for i in range(2)]
        xt = [ctx.sb(f"xt{i}", [128, TTW], F32, ph) for i in range(2)]
        pcol = ctx.sb("pcol", [128, 1], F32, ph)
        stats_r = ctx.sb("stats_r", [1, T], F32, ph)
        stats_v = ctx.sb("stats_v", [1, T], F32, ph)
        stats_m = ctx.sb("stats_m", [1, T], F32, ph)

        pe_ln = {}
        act_sq = {}
        for c in range(DC):
            slot = "xda" if c % 2 == 0 else "xdb"
            @blk.sync
            def _(e, c=c, slot=slot):
                if c >= 2:
                    ctx.wait(e, "pe", pe_ln[c - 2])
                ctx.inc(e.dma_start(out=xch[c % 2][:], in_=io["x"][c * 128:(c + 1) * 128, :]), slot, 16)
            x_at = ctx.c[slot]
            @blk.scalar
            def _(e, c=c, slot=slot, x_at=x_at):
                ctx.wait(e, slot, x_at)
                if c >= 2:
                    ctx.wait(e, "pe", pe_ln[c - 2])
                act_sq[c] = ctx.inc(e.activation(sqch[c % 2][:], xch[c % 2][:], AF.Square), "act")
            @blk.tensor
            def _(e, c=c):
                ctx.wait(e, "act", act_sq[c])
                for tt in range(NTT):
                    e.matmul(ps_big[0:1, tt * TTW:(tt + 1) * TTW], ones_col[:],
                             xch[c % 2][:, tt * TTW:(tt + 1) * TTW],
                             start=(c == 0), stop=(c == DC - 1))
                last = None
                for tt in range(NTT):
                    last = e.matmul(ps_big[32:33, tt * TTW:(tt + 1) * TTW], ones_col[:],
                                    sqch[c % 2][:, tt * TTW:(tt + 1) * TTW],
                                    start=(c == 0), stop=(c == DC - 1))
                pe_ln[c] = ctx.inc(last, "pe")

        @blk.vector
        def _(e):
            ctx.wait(e, "pe", pe_ln[DC - 1])
            ctx.fence(e, e.tensor_scalar(stats_r[:], ps_big[0:1, 0:T], 1.0 / D, None, ALU.mult))
            ctx.fence(e, e.tensor_scalar(stats_v[:], ps_big[32:33, 0:T], 1.0 / D, None, ALU.mult))
            ctx.fence(e, e.tensor_tensor(stats_m[:], stats_r[:], stats_r[:], ALU.mult))
            ctx.fence(e, e.tensor_sub(stats_v[:], stats_v[:], stats_m[:]))
            ctx.inc(e.tensor_scalar(stats_v[:], stats_v[:], 1e-5, None, ALU.add), "dve")
        @blk.scalar
        def _(e):
            ctx.wait(e, "dve", ctx.c["dve"])
            e.activation(stats_v[:], stats_v[:], AF.Ln)
            ctx.inc(e.activation(stats_v[:], stats_v[:], AF.Exp, scale=-0.5), "act")
        @blk.vector
        def _(e):
            ctx.wait(e, "act", ctx.c["act"])
            e.memset(pooledS[:], 0.0)
            e.memset(sumsqS[:], 0.0)
            ctx.inc(e.tensor_tensor(stats_m[:], stats_r[:], stats_v[:], ALU.mult), "dve")
        stats_done = ctx.snap()

        dve_p2 = {}
        act_p2 = {}
        pe_bc = {}
        dma_p2 = {}
        for k in range(NTT * DC):
            tt, c = divmod(k, DC)
            if c == 0:
                @blk.tensor
                def _(e, tt=tt, k=k):
                    ctx.wait(e, stats_done)
                    if k >= 1:
                        ctx.wait(e, "dve", dve_p2[k - 1])
                    e.matmul(ps_a[:, 0:TTW], ones_row[:], stats_v[0:1, tt * TTW:(tt + 1) * TTW], start=True, stop=True)
                    i = e.matmul(ps_b[:, 0:TTW], ones_row[:], stats_m[0:1, tt * TTW:(tt + 1) * TTW], start=True, stop=True)
                    pe_bc[tt] = ctx.inc(i, "pe")
            slot2 = "xda" if k % 2 == 0 else "xdb"
            @blk.sync
            def _(e, k=k, tt=tt, c=c, slot2=slot2):
                ctx.wait(e, stats_done)
                if k >= 2:
                    ctx.wait(e, "dve", dve_p2[k - 2])
                ctx.inc(e.dma_start(
                    out=xt[k % 2][:],
                    in_=io["x"][c * 128:(c + 1) * 128, tt * TTW:(tt + 1) * TTW]), slot2, 16)
            dma_p2[k] = (slot2, ctx.c[slot2])
            @blk.vector
            def _(e, k=k, tt=tt, c=c):
                ctx.wait(e, dma_p2[k][0], dma_p2[k][1])
                ctx.wait(e, "pe", pe_bc[tt])
                if k >= 2:
                    ctx.wait(e, "act", act_p2[k - 2])
                e.tensor_tensor(xnf[k % 2][:], xt[k % 2][:], ps_a[:, 0:TTW], ALU.mult)
                e.tensor_tensor(xnf[k % 2][:], xnf[k % 2][:], ps_b[:, 0:TTW], ALU.subtract)
                dst = A_xn[:, c * T + tt * TTW: c * T + (tt + 1) * TTW]
                dstlo = A_lo[:, c * T + tt * TTW: c * T + (tt + 1) * TTW]
                e.tensor_copy(dst, xnf[k % 2][:])
                e.tensor_tensor(dstlo, xnf[k % 2][:], dst, ALU.subtract)
                ctx.fence(e, e.reduce_sum(pcol[:], xnf[k % 2][:], axis=AX.X))
                e.tensor_tensor(pooledS[:, c:c + 1], pooledS[:, c:c + 1], pcol[:], ALU.add)
                if k >= 1:
                    cp = (k - 1) % DC
                    ctx.wait(e, "act", act_p2[k - 1])
                    ctx.fence(e, e.reduce_sum(pcol[:], sq2[(k - 1) % 2][:], axis=AX.X))
                    e.tensor_tensor(sumsqS[:, cp:cp + 1], sumsqS[:, cp:cp + 1], pcol[:], ALU.add)
                dve_p2[k] = ctx.inc(e.engine_nop(), "dve")
            @blk.scalar
            def _(e, k=k):
                ctx.wait(e, "dve", dve_p2[k])
                act_p2[k] = ctx.inc(e.activation(sq2[k % 2][:], xnf[k % 2][:], AF.Square), "act")
        @blk.vector
        def _(e):
            kk = NTT * DC - 1
            ctx.wait(e, "act", act_p2[kk])
            ctx.fence(e, e.reduce_sum(pcol[:], sq2[kk % 2][:], axis=AX.X))
            cp = kk % DC
            ctx.inc(e.tensor_tensor(sumsqS[:, cp:cp + 1], sumsqS[:, cp:cp + 1], pcol[:], ALU.add), "dve")
        ctx.barrier()
    ctx.tap("xn", A_xn[:], [128, DC * T], BF16)
    ctx.tap("pooledS", pooledS[:], [128, 8], F32)
    ctx.tap("sumsqS", sumsqS[:], [128, 8], F32)
    if upto <= 1:
        return

    with ExitStack() as ph:
        whi = ctx.sb("whi", [128, 9 * 1024], BF16, ph)
        wlo = ctx.sb("wlo", [128, 9 * 1024], BF16, ph)
        wstage = [ctx.sb(f"wst{i}", [128, 1024], F32, ph) for i in range(2)]
        h_f = ctx.sb("h_f", [128, 64], F32, ph)
        h_hi = ctx.sb("h_hi", [128, 64], BF16, ph)
        h_lo = ctx.sb("h_lo", [128, 64], BF16, ph)
        vf_scr = ctx.sb("vf_scr", [128, 8], F32, ph)
        sv_scr = ctx.sb("sv_scr", [1, 8], F32, ph)
        bcol = ctx.sb("bcol", [128, 8], F32, ph)
        b3col = ctx.sb("b3col", [3, 1], F32, ph)
        lgcol = ctx.sb("lgcol", [3, 1], F32, ph)
        r11 = ctx.sb("r11", [1, 1], F32, ph)
        alpha = ctx.sb("alpha", [1, 1], F32, ph)
        gma = ctx.sb("gma", [1, 1], F32, ph)
        apg = ctx.sb("apg", [1, 1], F32, ph)
        omb = ctx.sb("omb", [1, 1], F32, ph)
        t16a = ctx.sb("t16a", [1, H], F32, ph)
        t16b = ctx.sb("t16b", [1, H], F32, ph)

        @blk.vector
        def _(e):
            e.memset(h_f[:], 0.0)
            e.tensor_scalar(enh_f[:, 0:8], pooledS[:], 1.0 / T, None, ALU.mult)
            ctx.fence(e, e.tensor_tensor(vf_scr[:], pooledS[:], pooledS[:], ALU.mult))
            ctx.fence(e, e.tensor_scalar(vf_scr[:], vf_scr[:], 1.0 / T, None, ALU.mult))
            ctx.fence(e, e.tensor_tensor(vf_scr[:], vf_scr[:], sumsqS[:], ALU.subtract))
            ctx.inc(e.tensor_scalar(vf_scr[:], vf_scr[:], -1.0 / (T - 1), None, ALU.mult), "dve")
        @blk.tensor
        def _(e):
            ctx.wait(e, "dve", ctx.c["dve"])
            ctx.inc(e.matmul(ps_d[0:1, 0:8], ones_col[:], vf_scr[:], start=True, stop=True), "pe")
        @blk.vector
        def _(e):
            ctx.wait(e, "pe", ctx.c["pe"])
            ctx.fence(e, e.reduce_sum(r11[:], ps_d[0:1, 0:8], axis=AX.X))
            ctx.fence(e, e.tensor_scalar(r11[:], r11[:], 1.0 / D, None, ALU.mult))
            e.memset(enh_f[:, 8:9], 0.0)
            ctx.inc(e.memset(enh_f[0:1, 8:9], 4.0), "dve")
        ctx.tap("vf", vf_scr[:], [128, 8], F32)
        ctx.tap("r11", r11[:], [1, 1], F32)
        @blk.sync
        def _(e):
            ctx.wait(e, "dve", ctx.c["dve"])
            n = ctx.inc(e.dma_start(out=svar_dram.unsqueeze(0), in_=r11[:]), "xdma", 16)
            e.wait_ge(ctx.sems["xdma"], n)
            ctx.inc(e.dma_start(out=enh_f[1:2, 8:9], in_=svar_dram.unsqueeze(0)), "xdma", 16)
        @blk.vector
        def _(e):
            ctx.wait(e, "xdma", ctx.c["xdma"])
            e.memset(enh_f[:, 9:64], 0.0)
            e.tensor_copy(enh_hi[:], enh_f[:])
            ctx.inc(e.tensor_tensor(enh_lo[:], enh_f[:], enh_hi[:], ALU.subtract), "dve")

        def mlp_layer(w_dram, nin_rows, nout, in_hi, in_lo, bias_dram, relu, out_is_col3):
            nci = (nin_rows + 127) // 128
            nco = nout // 128 if not out_is_col3 else 1
            dvec0 = ctx.c["dve"]
            conv_at = {-2: dvec0, -1: dvec0}
            for ci in range(nci):
                r0, r1 = ci * 128, min(nin_rows, (ci + 1) * 128)
                slot = "wda" if ci % 2 == 0 else "wdb"
                @blk.sync
                def _(e, ci=ci, r0=r0, r1=r1, slot=slot):
                    ctx.wait(e, "dve", conv_at[ci - 2])
                    ctx.inc(e.dma_start(out=wstage[ci % 2][0:r1 - r0, 0:nout], in_=w_dram[r0:r1, :]), slot, 16)
                w_at = ctx.c[slot]
                @blk.vector
                def _(e, ci=ci, r0=r0, r1=r1, slot=slot, w_at=w_at):
                    ctx.wait(e, slot, w_at)
                    wc = max(nout, 64)
                    if r1 - r0 < 128 or out_is_col3:
                        e.memset(whi[:, ci * 1024:ci * 1024 + max(nout, 128)], 0.0)
                        e.memset(wlo[:, ci * 1024:ci * 1024 + max(nout, 128)], 0.0)
                    if nout < wc:
                        e.memset(wstage[ci % 2][0:r1 - r0, nout:wc], 0.0)
                    e.tensor_copy(whi[0:r1 - r0, ci * 1024:ci * 1024 + wc], wstage[ci % 2][0:r1 - r0, 0:wc])
                    ctx.inc(e.tensor_tensor(wlo[0:r1 - r0, ci * 1024:ci * 1024 + wc],
                                            wstage[ci % 2][0:r1 - r0, 0:wc],
                                            whi[0:r1 - r0, ci * 1024:ci * 1024 + wc], ALU.subtract), "dve")
                conv_at[ci] = ctx.c["dve"]
            conv_done = ctx.c["dve"]
            if bias_dram is not None and not out_is_col3:
                @blk.sync
                def _(e):
                    ctx.wait(e, "wdma", ctx.c["wdma"])
                    ctx.inc(e.dma_start(out=bcol[:, 0:nout // 128], in_=bias_dram[:, 0:nout // 128]), "wdma", 16)
            @blk.tensor
            def _(e):
                ctx.wait(e, "dve", conv_done)
                if out_is_col3:
                    last = None
                    for ci in range(nci):
                        for wa, xa in ((whi, in_hi), (whi, in_lo), (wlo, in_hi)):
                            last = e.matmul(ps_a[0:128, 0:1], wa[:, ci * 1024:ci * 1024 + 128],
                                            xa[:, ci:ci + 1], start=(ci == 0 and wa is whi and xa is in_hi),
                                            stop=(ci == nci - 1 and wa is wlo))
                    ctx.inc(last, "pe")
                else:
                    for co in range(nout // 128):
                        last = None
                        first = True
                        for ci in range(nci):
                            for wa, xa in ((whi, in_hi), (whi, in_lo), (wlo, in_hi)):
                                last = e.matmul(ps_a[:, co:co + 1],
                                                wa[:, ci * 1024 + co * 128:ci * 1024 + (co + 1) * 128],
                                                xa[:, ci:ci + 1], start=first, stop=(ci == nci - 1 and wa is wlo))
                                first = False
                        if co == nout // 128 - 1:
                            ctx.inc(last, "pe")
            @blk.vector
            def _(e):
                ctx.wait(e, "pe", ctx.c["pe"])
                ctx.wait(e, "wdma", ctx.c["wdma"])
                if out_is_col3:
                    e.tensor_copy(bcol[:, 0:1], ps_a[0:128, 0:1])
                    ctx.inc(e.tensor_tensor(lgcol[:], ps_a[0:3, 0:1], b3col[:], ALU.add), "dve")
                else:
                    nco_ = nout // 128
                    ctx.inc(e.tensor_tensor(h_f[:, 0:nco_], ps_a[:, 0:nco_], bcol[:, 0:nco_], ALU.add), "dve")
            if not out_is_col3:
                @blk.scalar
                def _(e):
                    ctx.wait(e, "dve", ctx.c["dve"])
                    nco_ = nout // 128
                    if relu:
                        ctx.inc(e.activation(h_f[:, 0:nco_], h_f[:, 0:nco_], AF.Relu), "act")
                    else:
                        ctx.inc(e.nop(), "act")
                @blk.vector
                def _(e):
                    ctx.wait(e, "act", ctx.c["act"])
                    e.tensor_copy(h_hi[:], h_f[:])
                    ctx.inc(e.tensor_tensor(h_lo[:], h_f[:], h_hi[:], ALU.subtract), "dve")

        ctx.tap("enh_f", enh_f[:], [128, 64], F32)
        ctx.tap("enh_hi", enh_hi[:], [128, 64], BF16)
        ctx.tap("enh_lo", enh_lo[:], [128, 64], BF16)
        mlp_layer(io["featw"], 1026, 1024, enh_hi, enh_lo, io["featb"], relu=False, out_is_col3=False)
        ctx.tap("hf1", h_f[:], [128, 64], F32)
        ctx.tap("whi_c0", whi[:, 0:1024], [128, 1024], BF16)
        ctx.tap("whi_c8", whi[:, 8 * 1024:9 * 1024], [128, 1024], BF16)
        ctx.tap("bcol1", bcol[:], [128, 8], F32)
        mlp_layer(io["ps1w"], 1024, 1024, h_hi, h_lo, io["ps1b"], relu=True, out_is_col3=False)
        ctx.tap("hf2", h_f[:], [128, 64], F32)
        mlp_layer(io["ps2w"], 1024, 512, h_hi, h_lo, io["ps2b"], relu=True, out_is_col3=False)
        ctx.tap("hf3", h_f[:], [128, 64], F32)
        @blk.sync
        def _(e):
            ctx.wait(e, "wdma", ctx.c["wdma"])
            ctx.inc(e.dma_start(out=b3col[:, 0:1], in_=io["ps3b"].unsqueeze(1)), "wdma", 16)
            ctx.inc(e.dma_start(out=lgcol[:, 0:1], in_=io["patb"].unsqueeze(1)), "cbs", 16)
        @blk.vector
        def _(e):
            ctx.wait(e, "wdma", ctx.c["wdma"])
            ctx.wait(e, "cbs", ctx.c["cbs"])
            ctx.inc(e.tensor_tensor(b3col[:], b3col[:], lgcol[:], ALU.add), "dve")
        ctx.tap("hhi3", h_hi[:], [128, 64], BF16)
        ctx.tap("hlo3", h_lo[:], [128, 64], BF16)
        mlp_layer(io["ps3w"], 512, 3, h_hi, h_lo, None, relu=False, out_is_col3=True)
        ctx.tap("lgcol", lgcol[:], [3, 1], F32)
        ctx.tap("ps3psum", bcol[:, 0:1], [128, 1], F32)

        @blk.sync
        def _(e):
            ctx.wait(e, ctx.snap())
            n = ctx.inc(e.dma_start(out=lg_dram.unsqueeze(1), in_=lgcol[:]), "xdma", 16)
            e.wait_ge(ctx.sems["xdma"], n)
            ctx.inc(e.dma_start(out=pw_row[0:1, :], in_=lg_dram.unsqueeze(0)), "xdma", 16)
        @blk.vector
        def _(e):
            ctx.wait(e, "xdma", ctx.c["xdma"])
            ctx.fence(e, e.reduce_max(r11[:], pw_row[:], axis=AX.X))
            ctx.inc(e.tensor_scalar(pw_row[:], pw_row[:], r11[0:1, 0:1], None, ALU.subtract), "dve")
            e.wait_ge(ctx.sems["dve"], ctx.c["dve"])
        @blk.scalar
        def _(e):
            ctx.wait(e, "dve", ctx.c["dve"])
            ctx.inc(e.activation(pw_row[:], pw_row[:], AF.Exp, scale=1.0 / 0.3), "act")
        @blk.vector
        def _(e):
            ctx.wait(e, "act", ctx.c["act"])
            ctx.fence(e, e.reduce_sum(r11[:], pw_row[:], axis=AX.X))
            ctx.fence(e, e.reciprocal(r11[:], r11[:]))
            ctx.fence(e, e.tensor_scalar(pw_row[:], pw_row[:], r11[0:1, 0:1], None, ALU.mult))
            ctx.fence(e, e.tensor_single_scalar(alpha[:], pw_row[0:1, 1:2], THRESH, ALU.is_gt))
            ctx.fence(e, e.tensor_tensor(apg[:], pw_row[0:1, 0:1], pw_row[0:1, 1:2], ALU.add))
            ctx.fence(e, e.tensor_single_scalar(omb[:], apg[:], THRESH, ALU.is_gt))
            ctx.fence(e, e.tensor_scalar(omb[:], omb[:], -1.0, 1.0, ALU.mult, ALU.add))
            ctx.fence(e, e.tensor_tensor(gma[:], pw_row[0:1, 1:2], pw_row[0:1, 2:3], ALU.add))
            ctx.fence(e, e.tensor_single_scalar(gma[:], gma[:], THRESH, ALU.is_gt))
            ctx.fence(e, e.tensor_tensor(gma[:], gma[:], alpha[:], ALU.subtract))
            ctx.fence(e, e.tensor_tensor(apg[:], alpha[:], gma[:], ALU.add))
            ctx.fence(e, e.tensor_single_scalar(t16a[:], srow[:], 0.0, ALU.is_gt))
            ctx.fence(e, e.tensor_scalar(t16b[:], t16a[:], 2.0, -1.0, ALU.mult, ALU.add))
            ctx.fence(e, e.tensor_scalar(rows64[0:1, 0:16], t16a[:], gma[0:1, 0:1], None, ALU.mult))
            ctx.fence(e, e.tensor_scalar(rows64[0:1, 0:16], rows64[0:1, 0:16], -1.0, apg[0:1, 0:1], ALU.mult, ALU.add))
            ctx.fence(e, e.tensor_scalar(rows64[0:1, 16:32], t16b[:], gma[0:1, 0:1], None, ALU.mult))
            ctx.fence(e, e.tensor_scalar(rows64[0:1, 32:48], t16a[:], omb[0:1, 0:1], None, ALU.mult))
            ctx.fence(e, e.tensor_scalar(rows64[0:1, 32:48], rows64[0:1, 32:48], -1.0, 1.0, ALU.mult, ALU.add))
            ctx.fence(e, e.tensor_scalar(rows64[0:1, 48:64], t16b[:], omb[0:1, 0:1], None, ALU.mult))
            ctx.fence(e, e.tensor_scalar(rows64m[0:1, 0:16], srow[:], 0.125, None, ALU.mult))
            ctx.fence(e, e.tensor_tensor(rows64m[0:1, 16:32], rows64m[0:1, 0:16], rows64m[0:1, 0:16], ALU.mult))
            ctx.fence(e, e.reciprocal(rows64m[0:1, 32:48], srow[:]))
            ctx.fence(e, e.tensor_scalar(t16a[:], rows64m[0:1, 32:48], -1.0, None, ALU.mult))
            ctx.inc(e.tensor_tensor(rows64m[0:1, 48:64], rows64m[0:1, 32:48], t16a[:], ALU.max), "dve")
        @blk.tensor
        def _(e):
            ctx.wait(e, "dve", ctx.c["dve"])
            e.matmul(ps_a[:, 0:64], ones_row[:], rows64[:], start=True, stop=True)
            ctx.inc(e.matmul(ps_b[:, 0:64], ones_row[:], rows64m[:], start=True, stop=True), "pe")
        @blk.scalar
        def _(e):
            ctx.wait(e, "pe", ctx.c["pe"])
            e.copy(col_uv[:], ps_a[:, 0:64])
            ctx.inc(e.copy(col_ms[:], ps_b[:, 0:64]), "act")
        ctx.barrier()
    ctx.tap("pw", pw_row[:], [1, 3], F32)
    ctx.tap("col_uv", col_uv[:], [128, 64], F32)
    ctx.tap("col_ms", col_ms[:], [128, 64], F32)
    if upto <= 2:
        return

    with ExitStack() as ph:
        gstage = ctx.sb("gstage", [128, 8, NE], F32, ph)
        ghi = ctx.sb("ghi", [128, 8 * NE], BF16, ph)
        glo = ctx.sb("glo", [128, 8 * NE], BF16, ph)
        erow = ctx.sb("erow", [1, NE], F32, ph)
        ebias_b = ctx.sb("ebias_b", [128, NE], F32, ph)
        sc_sb = ctx.sb("sc_sb", [128, NE], F32, ph)
        sfc_sb = ctx.sb("sfc_sb", [128, NE], F32, ph)
        gsc = [ctx.sb(f"gsc{i}", [128, 8], F32, ph) for i in range(2)]
        gmask = ctx.sb("gmask", [128, 8], F32, ph)
        mk = [ctx.sb(f"mk{i}", [128, NE], F32, ph) for i in range(2)]
        eq6 = ctx.sb("eq6", [128, 6 * NE], F32, ph)
        twc = ctx.sb("twc", [128, 8], F32, ph)
        cmb = [ctx.sb(f"cmb{i}", [128, NE], F32, ph) for i in range(2)]
        cmbt_t = ctx.sb("cmbt_t", [NE, 128], F32, ph)
        scr32 = ctx.sb("scr32", [128, NE], F32, ph)
        eqcol = ctx.sb("eqcol", [128, 1], F32, ph)

        @blk.sync
        def _(e):
            ctx.wait(e, "wdma", ctx.c["wdma"])
            ctx.inc(e.dma_start(out=gstage[:], in_=io["gatew"].rearrange("(c p) n -> p c n", p=128)), "wdma", 16)
            ctx.inc(e.dma_start(out=erow[0:1, :], in_=io["ebias"].unsqueeze(0)), "cbs", 16)
        @blk.vector
        def _(e):
            ctx.wait(e, "wdma", ctx.c["wdma"])
            ctx.wait(e, "cbs", ctx.c["cbs"])
            e.tensor_copy(ghi[:], gstage[:].rearrange("p c n -> p (c n)"))
            ctx.inc(e.tensor_tensor(glo[:], gstage[:].rearrange("p c n -> p (c n)"), ghi[:], ALU.subtract), "dve")
        @blk.tensor
        def _(e):
            ctx.wait(e, "dve", ctx.c["dve"])
            ctx.inc(e.matmul(ps_b[:, 0:NE], ones_row[:], erow[0:1, :], start=True, stop=True), "pe")
        @blk.scalar
        def _(e):
            ctx.wait(e, "pe", ctx.c["pe"])
            ctx.inc(e.copy(ebias_b[:], ps_b[:, 0:NE]), "act")
        route_ready = ctx.snap()

        rt_pe = {}
        rt_act = {}
        rt_dve = {}
        rt_store = {}
        for ti in range(QT):
            @blk.tensor
            def _(e, ti=ti):
                ctx.wait(e, route_ready)
                if ti >= 1:
                    ctx.wait(e, "act", rt_act[ti - 1])
                last = None
                first = True
                for ci in range(DC):
                    xhi = A_xn[:, ci * T + ti * 128: ci * T + ti * 128 + 128]
                    xlo = A_lo[:, ci * T + ti * 128: ci * T + ti * 128 + 128]
                    for wa, xa in ((ghi, xhi), (glo, xhi), (ghi, xlo)):
                        last = e.matmul(ps_a[:, 0:NE], xa, wa[:, ci * NE:(ci + 1) * NE],
                                        start=first, stop=(ci == DC - 1 and wa is ghi and xa is xlo))
                        first = False
                rt_pe[ti] = ctx.inc(last, "pe")
            @blk.scalar
            def _(e, ti=ti):
                ctx.wait(e, "pe", rt_pe[ti])
                if ti >= 1:
                    ctx.wait(e, "dve", rt_dve[ti - 1])
                rt_act[ti] = ctx.inc(e.activation(sc_sb[:], ps_a[:, 0:NE], AF.Sigmoid), "act")
            @blk.vector
            def _(e, ti=ti):
                ctx.wait(e, "act", rt_act[ti])
                ctx.fence(e, e.tensor_tensor(sfc_sb[:], sc_sb[:], ebias_b[:], ALU.add))
                ctx.fence(e, e.reduce_max(gsc[0][:], sfc_sb[:].rearrange("p (g f) -> p g f", g=8), axis=AX.X))
                e.memset(gmask[:], 0.0)
                for j in range(4):
                    ctx.fence(e, e.reduce_max(eqcol[:], gsc[j % 2][:], axis=AX.X))
                    ctx.fence(e, e.tensor_scalar(gsc[1 - j % 2][:], gsc[j % 2][:], eqcol[:], None, ALU.is_equal))
                    ctx.fence(e, e.tensor_tensor(gmask[:], gmask[:], gsc[1 - j % 2][:], ALU.add))
                    ctx.fence(e, e.scalar_tensor_tensor(gsc[1 - j % 2][:], gsc[1 - j % 2][:], -1e9, gsc[j % 2][:],
                                           ALU.mult, ALU.add))
                ctx.fence(e, e.tensor_tensor(mk[0][:].rearrange("p (g f) -> p g f", g=8),
                                sfc_sb[:].rearrange("p (g f) -> p g f", g=8),
                                gmask[:].unsqueeze(2).broadcast_to([128, 8, 4]), ALU.mult))
                for j in range(6):
                    ctx.fence(e, e.reduce_max(eqcol[:], mk[j % 2][:], axis=AX.X))
                    ctx.fence(e, e.tensor_scalar(eq6[:, j * NE:(j + 1) * NE], mk[j % 2][:], eqcol[:], None, ALU.is_equal))
                    ctx.fence(e, e.tensor_tensor(scr32[:], sc_sb[:], eq6[:, j * NE:(j + 1) * NE], ALU.mult))
                    ctx.fence(e, e.reduce_sum(twc[:, j:j + 1], scr32[:], axis=AX.X))
                    ctx.fence(e, e.scalar_tensor_tensor(mk[1 - j % 2][:], eq6[:, j * NE:(j + 1) * NE], -1e9, mk[j % 2][:],
                                           ALU.mult, ALU.add))
                ctx.fence(e, e.reduce_sum(twc[:, 6:7], twc[:, 0:6], axis=AX.X))
                ctx.fence(e, e.tensor_scalar(twc[:, 6:7], twc[:, 6:7], 1e-20, None, ALU.add))
                ctx.fence(e, e.reciprocal(twc[:, 7:8], twc[:, 6:7]))
                ctx.fence(e, e.tensor_scalar(twc[:, 7:8], twc[:, 7:8], 2.5, None, ALU.mult))
                e.memset(cmb[0][:], 0.0)
                for j in range(6):
                    ctx.fence(e, e.tensor_tensor(eqcol[:], twc[:, j:j + 1], twc[:, 7:8], ALU.mult))
                    ctx.fence(e, e.scalar_tensor_tensor(cmb[(j + 1) % 2][:], eq6[:, j * NE:(j + 1) * NE], eqcol[:],
                                           cmb[j % 2][:], ALU.mult, ALU.add))
                rt_dve[ti] = ctx.inc(e.engine_nop(), "dve")
            @blk.tensor
            def _(e, ti=ti):
                ctx.wait(e, "dve", rt_dve[ti])
                rt_pe[(ti, "tr")] = ctx.inc(e.transpose(ps_d[0:NE, 0:128], cmb[0][:], ident32[:]), "pe")
            @blk.scalar
            def _(e, ti=ti):
                ctx.wait(e, "pe", rt_pe[(ti, "tr")])
                if ti >= 1:
                    ctx.wait(e, "xdma", rt_store[ti - 1])
                rt_act[(ti, "tr")] = ctx.inc(e.copy(cmbt_t[:], ps_d[0:NE, 0:128]), "act")
            @blk.sync
            def _(e, ti=ti):
                ctx.wait(e, "act", rt_act[(ti, "tr")])
                ctx.wait(e, "xdma", ctx.c["xdma"])
                n = ctx.inc(e.dma_start(out=io["cmbT"][:, ti * 128:(ti + 1) * 128], in_=cmbt_t[:]), "xdma", 16)
                rt_store[ti] = n
                if ti == QT - 1:
                    e.wait_ge(ctx.sems["xdma"], n)
        ctx.barrier()
    ctx.tap("sc_last", sc_sb[:], [128, NE], F32)
    if upto <= 3:
        return

    moe_ph = ExitStack()
    gu_bf = ctx.sb("gu_bf", [128, 8 * 1024], BF16, moe_ph)
    dn_bf = ctx.sb("dn_bf", [128, 8 * 1024], BF16, moe_ph)
    shu_bf = ctx.sb("shu_bf", [128, 8 * 1024], BF16, moe_ph)
    wstage2 = [ctx.sb(f"wstage2_{i}", [128, 1024], F32, moe_ph) for i in range(2)]
    cb_row = ctx.sb("cb_row", [1, T], F32, moe_ph)
    cb_b = ctx.sb("cb_b", [128, T], BF16, moe_ph)
    hp_sb = ctx.sb("hp_sb", [128, 4096], BF16, moe_ph)
    hsg = ctx.sb("hsg", [128, TTW], BF16, moe_ph)
    tb_scr = ctx.sb("tb_scr", [128, TTW], BF16, moe_ph)

    @blk.vector
    def _(e):
        ctx.inc(e.memset(A_acc[:], 0.0), "dve")

    def stream_weights(dram_ap, nchunks, dst_bf, pe_free):
        d0 = ctx.c["dve"]
        conv_at = {-2: d0, -1: d0}
        for ci in range(nchunks):
            slot = "wda" if ci % 2 == 0 else "wdb"
            @blk.sync
            def _(e, ci=ci, slot=slot):
                ctx.wait(e, "dve", conv_at[ci - 2])
                ctx.inc(e.dma_start(out=wstage2[ci % 2][:], in_=dram_ap[ci * 128:(ci + 1) * 128, :]), slot, 16)
            w_at = ctx.c[slot]
            @blk.vector
            def _(e, ci=ci, slot=slot, w_at=w_at):
                ctx.wait(e, slot, w_at)
                if ci == 0:
                    ctx.wait(e, "pe", pe_free)
                ctx.inc(e.tensor_copy(dst_bf[:, ci * 1024:(ci + 1) * 1024], wstage2[ci % 2][:]), "dve")
            conv_at[ci] = ctx.c["dve"]
        return ctx.c["dve"]

    def expert_compute(nj, use_cb, gate_bf, up_bf, upcol0, down_bf, conv_done):
        for tt in range(NTT):
            for j in range(nj):
                @blk.tensor
                def _(e, tt=tt, j=j):
                    ctx.wait(e, "dve", conv_done)
                    if tt == 0 and j == 0:
                        ctx.wait(e, "act", ctx.c["act"])
                    else:
                        ctx.wait(e, "dve", ctx.c["dve"])
                    last = None
                    for ci in range(DC):
                        xs = A_xn[:, ci * T + tt * TTW: ci * T + (tt + 1) * TTW]
                        e.matmul(ps_a[:, 0:TTW], gate_bf[:, ci * 1024 + j * 128: ci * 1024 + (j + 1) * 128],
                                 xs, start=(ci == 0), stop=(ci == DC - 1))
                    for ci in range(DC):
                        xs = A_xn[:, ci * T + tt * TTW: ci * T + (tt + 1) * TTW]
                        last = e.matmul(ps_b[:, 0:TTW],
                                        up_bf[:, ci * 1024 + upcol0 + j * 128: ci * 1024 + upcol0 + (j + 1) * 128],
                                        xs, start=(ci == 0), stop=(ci == DC - 1))
                    ctx.inc(last, "pe")
                @blk.scalar
                def _(e, tt=tt, j=j):
                    ctx.wait(e, "pe", ctx.c["pe"])
                    ctx.inc(e.activation(hsg[:], ps_a[:, 0:TTW], AF.Silu), "act")
                @blk.vector
                def _(e, tt=tt, j=j):
                    ctx.wait(e, "act", ctx.c["act"])
                    ctx.wait(e, "gps", ctx.c["gps"])
                    ctx.inc(e.tensor_tensor(tb_scr[:], hsg[:], ps_b[:, 0:TTW], ALU.mult), "dve")
                @blk.gpsimd
                def _(e, tt=tt, j=j):
                    ctx.wait(e, "dve", ctx.c["dve"])
                    if use_cb:
                        ctx.inc(e.tensor_tensor(hp_sb[:, j * TTW:(j + 1) * TTW], tb_scr[:],
                                                cb_b[:, tt * TTW:(tt + 1) * TTW], ALU.mult), "gps")
                    else:
                        ctx.inc(e.tensor_copy(hp_sb[:, j * TTW:(j + 1) * TTW], tb_scr[:]), "gps")
            hp_done = ctx.c["gps"]
            for oc in range(DC):
                @blk.tensor
                def _(e, oc=oc, tt=tt):
                    ctx.wait(e, "gps", hp_done)
                    ctx.wait(e, "dve", ctx.c["dve"])
                    last = None
                    for j in range(nj):
                        last = e.matmul(ps_c[:, 0:TTW],
                                        down_bf[:, j * 1024 + oc * 128: j * 1024 + (oc + 1) * 128],
                                        hp_sb[:, j * TTW:(j + 1) * TTW],
                                        start=(j == 0), stop=(j == nj - 1))
                    ctx.inc(last, "pe")
                @blk.vector
                def _(e, oc=oc, tt=tt):
                    ctx.wait(e, "pe", ctx.c["pe"])
                    dst = A_acc[:, oc * T + tt * TTW: oc * T + (tt + 1) * TTW]
                    ctx.inc(e.tensor_tensor(dst, dst, ps_c[:, 0:TTW], ALU.add), "dve")

    import os as _os
    import os
    moe_ne = int(os.environ.get("MOE_NE", str(NE)))
    moe_sh = int(os.environ.get("MOE_SH", "1"))
    pe_guard = 0
    for ex in range(moe_ne):
        conv1 = stream_weights(io["gu"][ex], DC, gu_bf, pe_guard)
        conv2 = stream_weights(io["dn"][ex], 4, dn_bf, pe_guard)
        @blk.sync
        def _(e, ex=ex):
            ctx.wait(e, "pe", ctx.c["pe"])
            ctx.inc(e.dma_start(out=cb_row[0:1, :], in_=io["cmbT"][ex:ex + 1, :]), "cbs", 16)
        cbw = ctx.c["cbs"]
        for tt in range(NTT):
            @blk.tensor
            def _(e, tt=tt):
                ctx.wait(e, "cbs", cbw)
                ctx.wait(e, "dve", ctx.c["dve"])
                ctx.wait(e, "act", ctx.c["act"])
                ctx.inc(e.matmul(ps_d[:, 0:TTW], ones_row[:], cb_row[0:1, tt * TTW:(tt + 1) * TTW],
                                 start=True, stop=True), "pe")
            @blk.scalar
            def _(e, tt=tt):
                ctx.wait(e, "pe", ctx.c["pe"])
                ctx.inc(e.copy(cb_b[:, tt * TTW:(tt + 1) * TTW], ps_d[:, 0:TTW]), "act")
        expert_compute(4, not _os.environ.get("MOE_NOCB") if False else (_os.environ.get("MOE_NOCB") is None), gu_bf, gu_bf, 512, dn_bf, max(conv1, conv2))
        import os as _os
        if _os.environ.get("MOE_TAPHP") and ex == 0:
            ctx.tap("hp", hp_sb[:], [128, 4096], BF16)
            ctx.tap("cb_b", cb_b[:], [128, T], BF16)
        if _os.environ.get("MOE_BARRIER"):
            ctx.barrier()
        pe_guard = ctx.c["pe"]

    if not moe_sh:
        conv1 = None
    conv1 = stream_weights(io["shg"], DC, gu_bf, pe_guard) if moe_sh else None
    if moe_sh:
        conv2 = stream_weights(io["shu"], DC, shu_bf, pe_guard)
        conv3 = stream_weights(io["shd"], DC, dn_bf, pe_guard)
        expert_compute(8, False, gu_bf, shu_bf, 0, dn_bf, max(conv1, conv2, conv3))
    ctx.barrier()
    moe_ph.close()

    ctx.tap("accmoe", A_acc[:], [128, DC * T], BF16)
    A_g0 = sb("A_g0", [128, DC * T], BF16)
    for c in range(DC):
        @blk.scalar
        def _(e, c=c):
            ctx.inc(e.activation(A_g0[:, c * T:(c + 1) * T], A_acc[:, c * T:(c + 1) * T], AF.Silu), "act")
    ctx.barrier()
    ctx.tap("g0", A_g0[:], [128, DC * T], BF16)
    if upto <= 4:
        return

    A_v = sb("A_v", [128, KC * H * 65], FP16)
    kbar_row = sb("kbar_row", [1, 1024], BF16)
    qkv_ph = ExitStack()
    wq_bf = ctx.sb("wq_bf", [128, 8 * 1024], BF16, qkv_ph)
    wstage3 = [ctx.sb(f"wstage3_{i}", [128, 1024], F32, qkv_ph) for i in range(2)]
    ktok_scr = ctx.sb("ktok_scr", [128, 1024], BF16, qkv_ph)
    G_acc = ctx.sb("G_acc", [64, 1024], F32, qkv_ph)
    kb_acc = ctx.sb("kb_acc", [1, 1024], F32, qkv_ph)

    def stream_w3(dram_ap, col0, pe_free):
        d0 = ctx.c["dve"]
        conv_at = {-2: d0, -1: d0}
        for ci in range(DC):
            slot = "wda" if ci % 2 == 0 else "wdb"
            @blk.sync
            def _(e, ci=ci, slot=slot):
                ctx.wait(e, "dve", conv_at[ci - 2])
                ctx.inc(e.dma_start(out=wstage3[ci % 2][:],
                                    in_=dram_ap[ci * 128:(ci + 1) * 128, col0:col0 + 1024]), slot, 16)
            w_at = ctx.c[slot]
            @blk.vector
            def _(e, ci=ci, slot=slot, w_at=w_at):
                ctx.wait(e, slot, w_at)
                if ci == 0:
                    ctx.wait(e, "pe", pe_free)
                ctx.inc(e.tensor_copy(wq_bf[:, ci * 1024:(ci + 1) * 1024], wstage3[ci % 2][:]), "dve")
            conv_at[ci] = ctx.c["dve"]
        return ctx.c["dve"]

    conv = stream_w3(io["qkv"], 0, ctx.c["pe"])
    for hp in range(DC):
        for tt in range(NTT):
            @blk.tensor
            def _(e, hp=hp, tt=tt):
                ctx.wait(e, "dve", conv)
                ctx.wait(e, "act", ctx.c["act"])
                last = None
                for ci in range(DC):
                    last = e.matmul(ps_a[:, 0:TTW],
                                    wq_bf[:, ci * 1024 + hp * 128: ci * 1024 + (hp + 1) * 128],
                                    A_xn[:, ci * T + tt * TTW: ci * T + (tt + 1) * TTW],
                                    start=(ci == 0), stop=(ci == DC - 1))
                ctx.inc(last, "pe")
            @blk.scalar
            def _(e, hp=hp, tt=tt):
                ctx.wait(e, "pe", ctx.c["pe"])
                ctx.inc(e.copy(A_lo[:, hp * T + tt * TTW: hp * T + (tt + 1) * TTW], ps_a[:, 0:TTW]), "act")
    conv = stream_w3(io["qkv"], 1024, ctx.c["pe"])
    for hp in range(DC):
        for tt in range(NTT):
            @blk.tensor
            def _(e, hp=hp, tt=tt):
                ctx.wait(e, "dve", conv)
                ctx.wait(e, "act", ctx.c["act"])
                last = None
                for ci in range(DC):
                    last = e.matmul(ps_a[:, 0:TTW],
                                    wq_bf[:, ci * 1024 + hp * 128: ci * 1024 + (hp + 1) * 128],
                                    A_xn[:, ci * T + tt * TTW: ci * T + (tt + 1) * TTW],
                                    start=(ci == 0), stop=(ci == DC - 1))
                ctx.inc(last, "pe")
            @blk.scalar
            def _(e, hp=hp, tt=tt):
                ctx.wait(e, "pe", ctx.c["pe"])
                ctx.inc(e.copy(A_acc[:, hp * T + tt * TTW: hp * T + (tt + 1) * TTW], ps_a[:, 0:TTW]), "act")
    for kc in range(KC):
        @blk.tensor
        def _(e, kc=kc):
            ctx.wait(e, "dve", ctx.c["dve"])
            ctx.wait(e, "act", ctx.c["act"])
            last = None
            for half in range(2):
                pst = ps_a if half == 0 else ps_b
                for ci in range(DC):
                    last = e.matmul(pst[:, 0:TTW],
                             A_xn[:, ci * T + kc * 128: ci * T + (kc + 1) * 128],
                             wq_bf[:, ci * 1024 + half * 512: ci * 1024 + (half + 1) * 512],
                             start=(ci == 0), stop=(ci == DC - 1))
            ctx.inc(last, "pe")
        @blk.vector
        def _(e, kc=kc):
            ctx.wait(e, "pe", ctx.c["pe"])
            e.tensor_copy(ktok_scr[:, 0:512], ps_a[:, 0:TTW])
            ctx.inc(e.tensor_copy(ktok_scr[:, 512:1024], ps_b[:, 0:TTW]), "dve")
        @blk.tensor
        def _(e, kc=kc):
            ctx.wait(e, "dve", ctx.c["dve"])
            for h in range(H):
                e.matmul(ps_big[0:64, h * 64:(h + 1) * 64], ktok_scr[:, h * 64:(h + 1) * 64],
                         ktok_scr[:, h * 64:(h + 1) * 64], start=True, stop=True)
            e.matmul(ps_big[64:65, 1024:1536], ones_colb[:],
                     ktok_scr[:, 0:512], start=True, stop=True)
            ctx.inc(e.matmul(ps_big[64:65, 1536:2048], ones_colb[:],
                             ktok_scr[:, 512:1024], start=True, stop=True), "pe")
        @blk.vector
        def _(e, kc=kc):
            ctx.wait(e, "pe", ctx.c["pe"])
            if kc == 0:
                e.memset(G_acc[:], 0.0)
                e.memset(kb_acc[:], 0.0)
            e.tensor_tensor(G_acc[:], G_acc[:], ps_big[0:64, 0:1024], ALU.add)
            ctx.inc(e.tensor_tensor(kb_acc[:], kb_acc[:], ps_big[64:65, 1024:2048], ALU.add), "dve")
    @blk.scalar
    def _(e):
        ctx.wait(e, "dve", ctx.c["dve"])
        for h in range(H):
            e.activation(G_sb[(h % 2) * 64:(h % 2) * 64 + 64, (h // 2) * 64:(h // 2) * 64 + 64],
                         G_acc[0:64, h * 64:(h + 1) * 64], AF.Copy, scale=1.0 / T)
        ctx.inc(e.activation(kbar_row[0:1, :], kb_acc[0:1, :], AF.Copy, scale=1.0 / T), "act")
    for ci in range(DC):
        @blk.tensor
        def _(e, ci=ci):
            ctx.wait(e, "act", ctx.c["act"])
            ctx.inc(e.transpose(ps_d[:, 0:1].bitcast(BF16)[:, 0:1], kbar_row[0:1, ci * 128:(ci + 1) * 128],
                                identb[0:1, 0:1]), "pe")
        @blk.scalar
        def _(e, ci=ci):
            ctx.wait(e, "pe", ctx.c["pe"])
            ctx.inc(e.copy(kbar_cols[:, ci:ci + 1], ps_d[:, 0:1].bitcast(BF16)[:, 0:1]), "act")
    conv = stream_w3(io["qkv"], 2048, ctx.c["pe"])
    for kc in range(KC):
        @blk.tensor
        def _(e, kc=kc):
            ctx.wait(e, "dve", max(conv, ctx.c["dve"]))
            ctx.wait(e, "act", ctx.c["act"])
            last = None
            for half in range(2):
                pst = ps_a if half == 0 else ps_b
                for ci in range(DC):
                    last = e.matmul(pst[:, 0:TTW],
                             A_xn[:, ci * T + kc * 128: ci * T + (kc + 1) * 128],
                             wq_bf[:, ci * 1024 + half * 512: ci * 1024 + (half + 1) * 512],
                             start=(ci == 0), stop=(ci == DC - 1))
            ctx.inc(last, "pe")
        @blk.vector
        def _(e, kc=kc):
            ctx.wait(e, "pe", ctx.c["pe"])
            v4 = A_v[:].rearrange("p (kc h c) -> p kc h c", kc=KC, h=H)
            e.tensor_copy(v4[:, kc, 0:8, 0:64], ps_a[:, 0:TTW].rearrange("p (h c) -> p h c", h=8))
            ctx.inc(e.tensor_copy(v4[:, kc, 8:16, 0:64], ps_b[:, 0:TTW].rearrange("p (h c) -> p h c", h=8)), "dve")
    @blk.vector
    def _(e):
        v4 = A_v[:].rearrange("p (kc h c) -> p kc h c", kc=KC, h=H)
        ctx.inc(e.memset(v4[:, :, :, 64:65], 1.0), "dve")
    ctx.barrier()
    qkv_ph.close()
    ctx.tap("q", A_lo[:], [128, DC * T], BF16)
    ctx.tap("k", A_acc[:], [128, DC * T], BF16)
    ctx.tap("vtok", A_v[:, 0:2080], [128, 2080], FP16)
    ctx.tap("G", G_sb[:], [128, 8 * 64], BF16)
    ctx.tap("kbar", kbar_cols[:], [128, 8], BF16)
    if upto <= 5:
        return

    att_ph = ExitStack()
    s_tok = ctx.sb("s_tok", [128, T], FP16, att_ph)
    p_sb = ctx.sb("p_sb", [128, T], FP16, att_ph)
    ind_sb = ctx.sb("ind_sb", [128, T], FP16, att_ph)
    qtok_sb = ctx.sb("qtok_sb", [128, 64], BF16, att_ph)
    ctx_sb = ctx.sb("ctx_sb", [64, 128], BF16, att_ph)
    scrb = [ctx.sb(f"scrb{i}", [128, 128], FP16, att_ph) for i in range(3)]

    ST_MU, ST_QG, ST_V, ST_LN, ST_SIG, ST_RSIG, ST_T0, ST_N0, ST_T1, ST_N1, \
        ST_TS, ST_TAU, ST_MH, ST_EMH, S14, S15, S16 = range(17)

    def stc(i):
        return state[:, i:i + 1]

    prev = {"a_exp": 0, "v_mask": 0, "v_n1": 0, "v_y": 0, "p_av": 0}
    for h in range(H):
        qb = (h % 2) * 64
        hp = h // 2
        q_all = A_lo[qb:qb + 64, hp * T:(hp + 1) * T]
        k_all = A_acc[qb:qb + 64, hp * T:(hp + 1) * T]
        for qt in range(QT):
            q_sl = A_lo[qb:qb + 64, hp * T + qt * 128: hp * T + (qt + 1) * 128]
            @blk.tensor
            def _(e, h=h, qt=qt, q_sl=q_sl):
                ctx.wait(e, "act", prev["a_exp"])
                ctx.wait(e, "dve", prev["v_mask"])
                for tt in range(NTT):
                    e.matmul(ps_big[:, tt * TTW:(tt + 1) * TTW], q_sl,
                             A_acc[qb:qb + 64, hp * T + tt * TTW: hp * T + (tt + 1) * TTW],
                             start=True, stop=True)
                e.matmul(ps_a[:, 0:64], q_sl,
                         G_sb[qb:qb + 64, hp * 64:(hp + 1) * 64], start=True, stop=True)
                idsl = identb[qb:qb + 64, qb:qb + 64]
                ctx.inc(e.transpose(ps_a[:, 160:192].bitcast(BF16), q_sl, idsl), "pe")
            p1 = ctx.c["pe"]
            @blk.scalar
            def _(e, h=h, p1=p1):
                ctx.wait(e, "pe", p1)
                ctx.wait(e, "dve", prev["v_n1"])
                e.activation(s_tok[:], ps_big[:, 0:T], AF.Copy, scale=col_ms[:, h:h + 1],
                             accum_out=stc(ST_MU))
                ctx.inc(e.copy(qtok_sb[:], ps_a[:, 160:192].bitcast(BF16)), "act")
            a1 = ctx.c["act"]
            @blk.vector
            def _(e, h=h, a1=a1):
                ctx.wait(e, "act", a1)
                ctx.fence(e, e.tensor_tensor(scrb[0][:, 0:64], qtok_sb[:], ps_a[:, 0:64], ALU.mult))
                ctx.fence(e, e.reduce_sum(stc(ST_QG), scrb[0][:, 0:64], axis=AX.X))
                ctx.fence(e, e.tensor_scalar(stc(ST_MU), stc(ST_MU), 1.0 / T, None, ALU.mult))
                ctx.fence(e, e.tensor_scalar(stc(ST_V), stc(ST_QG), col_ms[:, 16 + h:17 + h], None, ALU.mult))
                ctx.fence(e, e.tensor_tensor(stc(S14), stc(ST_MU), stc(ST_MU), ALU.mult))
                ctx.fence(e, e.tensor_tensor(stc(ST_V), stc(ST_V), stc(S14), ALU.subtract))
                ctx.inc(e.tensor_scalar(stc(ST_V), stc(ST_V), 1e-12, None, ALU.max), "dve")
            v1 = ctx.c["dve"]
            @blk.scalar
            def _(e, v1=v1):
                ctx.wait(e, "dve", v1)
                ctx.fence(e, e.activation(stc(ST_LN), stc(ST_V), AF.Ln), "act")
                e.activation(stc(ST_SIG), stc(ST_LN), AF.Exp, scale=0.5)
                ctx.inc(e.activation(stc(ST_RSIG), stc(ST_LN), AF.Exp, scale=-0.5), "act")
            a2 = ctx.c["act"]
            @blk.vector
            def _(e, h=h, a2=a2):
                ctx.wait(e, "act", a2)
                ctx.fence(e, e.tensor_tensor(stc(S14), stc(ST_MU), col_ms[:, 32 + h:33 + h], ALU.mult))
                ctx.fence(e, e.tensor_tensor(stc(S15), stc(ST_SIG), col_ms[:, 48 + h:49 + h], ALU.mult))
                ctx.fence(e, e.scalar_tensor_tensor(stc(ST_MH), stc(S15), 6.0, stc(S14), ALU.mult, ALU.add))
                ctx.inc(e.tensor_scalar(stc(ST_T0), stc(ST_SIG), Z70, stc(ST_MU), ALU.mult, ALU.add), "dve")
            v2 = ctx.c["dve"]
            @blk.scalar
            def _(e, v2=v2):
                ctx.wait(e, "dve", v2)
                ctx.inc(e.activation(stc(ST_EMH), stc(ST_MH), AF.Exp), "act")
            a3 = ctx.c["act"]
            @blk.tensor
            def _(e, a3=a3):
                ctx.wait(e, "act", a3)
                ctx.inc(e.matmul(ps_a[0:1, 64:65], ones_col[:], stc(ST_EMH), start=True, stop=True), "pe")
            p2 = ctx.c["pe"]
            @blk.scalar
            def _(e, p2=p2):
                ctx.wait(e, "pe", p2)
                ctx.inc(e.activation(mh_scr[:], ps_a[0:1, 64:65], AF.Ln), "act")
            a4 = ctx.c["act"]
            @blk.tensor
            def _(e, a4=a4):
                ctx.wait(e, "act", a4)
                ctx.inc(e.matmul(ps_a[:, 65:66], ones_row[:], mh_scr[:], start=True, stop=True), "pe")
            p3 = ctx.c["pe"]
            @blk.scalar
            def _(e, p3=p3):
                ctx.wait(e, "pe", p3)
                ctx.inc(e.activation(negmh[:], ps_a[:, 65:66], AF.Copy, scale=-1.0), "act")
            a5 = ctx.c["act"]
            @blk.vector
            def _(e, a1=a1):
                ctx.wait(e, "act", a1)
                ctx.fence(e, e.tensor_scalar(ind_sb[:], s_tok[:], stc(ST_T0), None, ALU.is_ge, ALU.add,
                                accum_out=stc(ST_N0)))
                ctx.fence(e, e.tensor_tensor(stc(S14), stc(ST_T0), stc(ST_MU), ALU.subtract))
                ctx.fence(e, e.tensor_tensor(stc(S14), stc(S14), stc(ST_RSIG), ALU.mult))
                ctx.inc(e.tensor_tensor(stc(S15), stc(S14), stc(S14), ALU.mult), "dve")
            v3 = ctx.c["dve"]
            @blk.scalar
            def _(e, v3=v3):
                ctx.wait(e, "dve", v3)
                ctx.inc(e.activation(stc(S15), stc(S15), AF.Exp, scale=-0.5), "act")
            a6 = ctx.c["act"]
            @blk.vector
            def _(e, h=h, a6=a6, a5=a5):
                ctx.wait(e, "act", a6)
                ctx.fence(e, e.tensor_tensor(stc(S15), stc(S15), stc(ST_RSIG), ALU.mult))
                ctx.fence(e, e.tensor_scalar(stc(S15), stc(S15), T * 0.3989423, None, ALU.mult))
                ctx.fence(e, e.reciprocal(stc(S15), stc(S15)))
                ctx.fence(e, e.tensor_scalar(stc(S14), stc(ST_N0), -1.0, float(KTOP), ALU.mult, ALU.add))
                ctx.fence(e, e.tensor_tensor(stc(S16), stc(S14), stc(S15), ALU.mult))
                ctx.fence(e, e.tensor_tensor(stc(ST_T1), stc(ST_T0), stc(S16), ALU.add))
                ctx.fence(e, e.tensor_scalar(ind_sb[:], s_tok[:], stc(ST_T1), None, ALU.is_ge, ALU.add,
                                accum_out=stc(ST_N1)))
                ctx.fence(e, e.tensor_tensor(stc(S14), stc(ST_N1), stc(ST_N0), ALU.subtract))
                ctx.fence(e, e.tensor_scalar(stc(S15), stc(S14), -1.0, None, ALU.mult))
                ctx.fence(e, e.tensor_tensor(stc(S15), stc(S15), stc(S14), ALU.max))
                ctx.fence(e, e.tensor_scalar(stc(S15), stc(S15), 0.5, 1e9, ALU.is_lt, ALU.mult))
                ctx.fence(e, e.tensor_tensor(stc(S14), stc(S14), stc(S15), ALU.add))
                ctx.fence(e, e.reciprocal(stc(S14), stc(S14)))
                ctx.fence(e, e.tensor_scalar(stc(S15), stc(ST_N1), -1.0, float(KTOP), ALU.mult, ALU.add))
                ctx.fence(e, e.tensor_tensor(stc(S16), stc(ST_T1), stc(ST_T0), ALU.subtract))
                ctx.fence(e, e.tensor_tensor(stc(S15), stc(S15), stc(S16), ALU.mult))
                ctx.fence(e, e.tensor_tensor(stc(S15), stc(S15), stc(S14), ALU.mult))
                ctx.fence(e, e.tensor_tensor(stc(ST_TS), stc(ST_T1), stc(S15), ALU.add))
                ctx.fence(e, e.tensor_tensor(stc(ST_TAU), stc(ST_TS), col_ms[:, 32 + h:33 + h], ALU.mult))
                ctx.wait(e, "act", a5)
                ctx.inc(e.tensor_tensor(stc(ST_TAU), stc(ST_TAU), negmh[:], ALU.add), "dve")
            v4 = ctx.c["dve"]
            @blk.scalar
            def _(e, v4=v4):
                ctx.wait(e, "dve", v4)
                ctx.inc(e.activation(stc(ST_TAU), stc(ST_TAU), AF.Exp), "act")
            a7 = ctx.c["act"]
            @blk.tensor
            def _(e, a7=a7):
                ctx.wait(e, "act", a7)
                ctx.inc(e.transpose(ps_a[0:1, 384:512], stc(ST_TAU), ident32[:]), "pe")
            p4 = ctx.c["pe"]
            @blk.scalar
            def _(e, p4=p4):
                ctx.wait(e, "pe", p4)
                ctx.inc(e.copy(tau_row[:], ps_a[0:1, 384:512]), "act")
            a8 = ctx.c["act"]
            @blk.tensor
            def _(e, h=h, qt=qt, a8=a8, a1=a1, q_sl=q_sl):
                ctx.wait(e, "act", a8)
                e.matmul(ps_a[:, 128:256], ones_row[:], tau_row[:], start=True, stop=True)
                ctx.wait(e, "act", a1)
                last = None
                for kc in range(KC):
                    last = e.matmul(ps_big[:, kc * 128:(kc + 1) * 128],
                             A_acc[qb:qb + 64, hp * T + kc * 128: hp * T + (kc + 1) * 128],
                             q_sl, start=True, stop=True)
                ctx.inc(last, "pe")
            p6 = ctx.c["pe"]
            @blk.scalar
            def _(e, p6=p6):
                ctx.wait(e, "pe", p6)
                ctx.wait(e, "pe", prev["p_av"])
                for kc in range(KC):
                    e.activation(p_sb[:, kc * 128:(kc + 1) * 128], ps_big[:, kc * 128:(kc + 1) * 128],
                                 AF.Exp, scale=0.125, bias=negmh[:])
                ctx.inc(e.nop(), "act")
            a9 = ctx.c["act"]
            @blk.vector
            def _(e, h=h, qt=qt, a9=a9):
                ctx.wait(e, "act", a9)
                p3d = p_sb[:].rearrange("p (kc q) -> p kc q", kc=KC)
                tb = ps_a[:, 128:256].unsqueeze(1).broadcast_to([128, KC, 128])
                e.tensor_tensor(ind_sb[:].rearrange("p (kc q) -> p kc q", kc=KC), p3d, tb, ALU.is_ge)
                bandlist = []
                if qt > 0:
                    bandlist.append((qt - 1, bandmf))
                bandlist.append((qt, bandcf))
                if qt < QT - 1:
                    bandlist.append((qt + 1, bandpf))
                for i, (bc, btile) in enumerate(bandlist):
                    e.tensor_scalar(scrb[i][:], ind_sb[:, bc * 128:(bc + 1) * 128],
                                    col_uv[:, 48 + h:49 + h], col_uv[:, 32 + h:33 + h], ALU.mult, ALU.add)
                e.tensor_scalar(ind_sb[:], ind_sb[:], col_uv[:, 16 + h:17 + h],
                                col_uv[:, 0 + h:1 + h], ALU.mult, ALU.add)
                for i, (bc, btile) in enumerate(bandlist):
                    sl = ind_sb[:, bc * 128:(bc + 1) * 128]
                    e.tensor_tensor(scrb[i][:], scrb[i][:], sl, ALU.subtract)
                    e.tensor_tensor(scrb[i][:], scrb[i][:], btile[:], ALU.mult)
                    e.tensor_tensor(sl, sl, scrb[i][:], ALU.add)
                ctx.inc(e.tensor_tensor(p_sb[:], p_sb[:], ind_sb[:], ALU.mult), "dve")
            v5 = ctx.c["dve"]
            @blk.tensor
            def _(e, h=h, v5=v5):
                ctx.wait(e, "dve", v5)
                last = None
                for kc in range(KC):
                    last = e.matmul(ps_c[0:65, 0:128],
                                    A_v[:, kc * H * 65 + h * 65: kc * H * 65 + (h + 1) * 65],
                                    p_sb[:, kc * 128:(kc + 1) * 128],
                                    start=(kc == 0), stop=(kc == KC - 1))
                ctx.inc(last, "pe")
            p7 = ctx.c["pe"]
            prev["p_av"] = p7
            @blk.scalar
            def _(e, p7=p7):
                ctx.wait(e, "pe", p7)
                ctx.wait(e, "dve", prev["v_y"])
                e.activation(zrow[:], ps_c[64:65, 0:128], AF.Ln)
                e.activation(zrow[:], zrow[:], AF.Exp, scale=-1.0)
                ctx.inc(e.copy(ctx_sb[:], ps_c[0:64, 0:128]), "act")
            a10 = ctx.c["act"]
            @blk.tensor
            def _(e, a10=a10):
                ctx.wait(e, "act", a10)
                ctx.inc(e.matmul(ps_d[0:64, 0:128], ones_row[0:1, 0:64], zrow[:], start=True, stop=True), "pe")
            p8 = ctx.c["pe"]
            @blk.vector
            def _(e, h=h, qt=qt, p8=p8):
                ctx.wait(e, "pe", p8)
                ydst = A_xn[qb:qb + 64, hp * T + qt * 128: hp * T + (qt + 1) * 128]
                ctx.inc(e.tensor_tensor(ydst, ctx_sb[:], ps_d[0:64, 0:128], ALU.mult), "dve")
            prev["v_y"] = ctx.c["dve"]
            prev["a_exp"] = a9
            prev["v_mask"] = v5
            prev["v_n1"] = v4
    ctx.barrier()
    att_ph.close()
    ctx.tap("y", A_xn[:], [128, DC * T], BF16)
    if upto <= 6:
        return

    proj_ph = ExitStack()
    pw_bf = ctx.sb("pw_bf", [128, 8 * 1024], BF16, proj_ph)
    pstage = [ctx.sb(f"pstage{i}", [128, 1024], F32, proj_ph) for i in range(2)]
    pb_cols = ctx.sb("pb_cols", [128, 8], F32, proj_ph)
    s_scr = ctx.sb("s_scr", [128, TTW], F32, proj_ph)
    xt2 = [ctx.sb(f"xt2_{i}", [128, TTW], F32, proj_ph) for i in range(2)]
    out_sb = [ctx.sb(f"outsb{i}", [128, TTW], F32, proj_ph) for i in range(2)]

    d0 = ctx.c["dve"]
    conv_at = {-2: d0, -1: d0}
    for ci in range(DC):
        slot = "wda" if ci % 2 == 0 else "wdb"
        @blk.sync
        def _(e, ci=ci, slot=slot):
            ctx.wait(e, "dve", conv_at[ci - 2])
            ctx.inc(e.dma_start(out=pstage[ci % 2][:], in_=io["projw"][ci * 128:(ci + 1) * 128, :]), slot, 16)
        w_at = ctx.c[slot]
        @blk.vector
        def _(e, ci=ci, slot=slot, w_at=w_at):
            ctx.wait(e, slot, w_at)
            ctx.inc(e.tensor_copy(pw_bf[:, ci * 1024:(ci + 1) * 1024], pstage[ci % 2][:]), "dve")
        conv_at[ci] = ctx.c["dve"]
    conv = ctx.c["dve"]
    @blk.sync
    def _(e):
        ctx.wait(e, "wdma", ctx.c["wdma"])
        ctx.inc(e.dma_start(out=pb_cols[:], in_=io["projb"]), "wdma", 16)
    pbw = ctx.c["wdma"]

    res_dve = {}
    res_x_at = {}
    res_store = {}
    for k in range(DC * NTT):
        oc, tt = divmod(k, NTT)
        slot3 = "xda" if k % 2 == 0 else "xdb"
        @blk.sync
        def _(e, k=k, oc=oc, tt=tt, slot3=slot3):
            if k >= 2:
                ctx.wait(e, "dve", res_dve[k - 2])
            ctx.inc(e.dma_start(out=xt2[k % 2][:],
                                in_=io["x"][oc * 128:(oc + 1) * 128, tt * TTW:(tt + 1) * TTW]), slot3, 16)
        res_x_at[k] = (slot3, ctx.c[slot3])
        @blk.tensor
        def _(e, k=k, oc=oc, tt=tt):
            ctx.wait(e, "dve", max(conv, ctx.c["dve"]))
            last = None
            for ci in range(DC):
                last = e.matmul(ps_a[:, 0:TTW], pw_bf[:, ci * 1024 + oc * 128: ci * 1024 + (oc + 1) * 128],
                                A_xn[:, ci * T + tt * TTW: ci * T + (tt + 1) * TTW],
                                start=(ci == 0), stop=(ci == DC - 1))
            ctx.inc(last, "pe")
        @blk.vector
        def _(e, k=k, oc=oc, tt=tt):
            ctx.wait(e, "pe", ctx.c["pe"])
            ctx.wait(e, "wdma", pbw)
            ctx.wait(e, res_x_at[k][0], res_x_at[k][1])
            if k >= 2:
                ctx.wait(e, res_store[k - 2][0], res_store[k - 2][1])
            e.tensor_scalar(s_scr[:], ps_a[:, 0:TTW], pb_cols[:, oc:oc + 1], None, ALU.add)
            e.tensor_tensor(out_sb[k % 2][:], s_scr[:], A_g0[:, oc * T + tt * TTW: oc * T + (tt + 1) * TTW], ALU.mult)
            res_dve[k] = ctx.inc(e.tensor_tensor(out_sb[k % 2][:], out_sb[k % 2][:], xt2[k % 2][:], ALU.add), "dve")
        slot4 = "sta" if k % 2 == 0 else "stb"
        @blk.sync
        def _(e, k=k, oc=oc, tt=tt, slot4=slot4):
            ctx.wait(e, "dve", res_dve[k])
            ctx.inc(e.dma_start(out=io["x1"][oc * 128:(oc + 1) * 128, tt * TTW:(tt + 1) * TTW],
                                in_=out_sb[k % 2][:]), slot4, 16)
        res_store[k] = (slot4, ctx.c[slot4])
    @blk.sync
    def _(e):
        e.wait_ge(ctx.sems["sta"], ctx.c["sta"])
        e.wait_ge(ctx.sems["stb"], ctx.c["stb"])
    ctx.barrier()
    proj_ph.close()


def build_block0(dbg=False, upto=99):
    nc = bass.Bass("TRN2", target_bir_lowering=False)
    d = {}
    d["x"] = nc.dram_tensor("x", [D, T], F32, kind="ExternalInput").ap()
    d["qkv"] = nc.dram_tensor("qkv_w_fm", [D, 3 * D], F32, kind="ExternalInput").ap()
    d["projw"] = nc.dram_tensor("proj_w_fm", [D, D], F32, kind="ExternalInput").ap()
    d["projb"] = nc.dram_tensor("proj_b_cols", [128, 8], F32, kind="ExternalInput").ap()
    d["featw"] = nc.dram_tensor("feat_w_fm", [1026, D], F32, kind="ExternalInput").ap()
    d["featb"] = nc.dram_tensor("feat_b_cols", [128, 8], F32, kind="ExternalInput").ap()
    d["ps1w"] = nc.dram_tensor("ps1_w_fm", [D, D], F32, kind="ExternalInput").ap()
    d["ps1b"] = nc.dram_tensor("ps1_b_cols", [128, 8], F32, kind="ExternalInput").ap()
    d["ps2w"] = nc.dram_tensor("ps2_w_fm", [D, 512], F32, kind="ExternalInput").ap()
    d["ps2b"] = nc.dram_tensor("ps2_b_cols", [128, 4], F32, kind="ExternalInput").ap()
    d["ps3w"] = nc.dram_tensor("ps3_w_fm", [512, 3], F32, kind="ExternalInput").ap()
    d["ps3b"] = nc.dram_tensor("ps3_b", [3], F32, kind="ExternalInput").ap()
    d["patb"] = nc.dram_tensor("pattern_bias", [3], F32, kind="ExternalInput").ap()
    d["spw"] = nc.dram_tensor("sparse_w", [H], F32, kind="ExternalInput").ap()
    d["gatew"] = nc.dram_tensor("gate_w_fm", [D, NE], F32, kind="ExternalInput").ap()
    d["ebias"] = nc.dram_tensor("e_bias", [NE], F32, kind="ExternalInput").ap()
    d["gu"] = nc.dram_tensor("gu_fm", [NE, D, D], F32, kind="ExternalInput").ap()
    d["dn"] = nc.dram_tensor("dn_fm", [NE, EI, D], F32, kind="ExternalInput").ap()
    d["shg"] = nc.dram_tensor("shg_fm", [D, D], F32, kind="ExternalInput").ap()
    d["shu"] = nc.dram_tensor("shu_fm", [D, D], F32, kind="ExternalInput").ap()
    d["shd"] = nc.dram_tensor("shd_fm", [D, D], F32, kind="ExternalInput").ap()
    d["x1"] = nc.dram_tensor("x1", [D, T], F32, kind="ExternalOutput").ap()
    d["cmbT"] = nc.dram_tensor("cmbT_scr", [NE, T], F32, kind="ExternalOutput" if dbg else "Internal").ap()
    outer = ExitStack()
    with nc.Block() as block:
        ctx = Ctx(nc, block, outer, dbg=dbg)
        emit_block0(ctx, d, upto=upto)
    outer.close()
    return nc



def emit_block1(ctx, io, upto=99):
    nc, blk = ctx.nc, FencingBlock(ctx, ctx.block)
    sb, ps = ctx.sb, ctx.ps

    A_xn = sb("A_xn", [128, DC * T], BF16)
    A_y = sb("A_y", [128, DC * T], BF16)
    q1_fm = sb("q1_fm", [128, DC * T], BF16)
    ones_col = sb("ones_col", [128, 1], F32)
    ones_colh = sb("ones_colh", [128, 1], FP16)
    ones_colb = sb("ones_colb", [128, 1], BF16)
    ones_row = sb("ones_row", [1, 128], F32)
    identb = sb("identb", [128, 128], BF16)
    tril32 = sb("tril32", [128, 128], F32)
    trilf = sb("trilf", [128, 128], FP16)
    ckv_fm = sb("ckv_fm", [128, T], BF16)
    ckv_tok = sb("ckv_tok", [128, KC * 128], FP16)
    ckvbar_col = sb("ckvbar_col", [128, 1], BF16)
    G1_sb = sb("G1_sb", [128, 128], BF16)
    keff_sb = sb("keff_sb", [128, 8 * 128], BF16)
    veff_sb = sb("veff_sb", [128, 1024], BF16)
    scols2 = sb("scols2", [128, 3], F32)
    srow3 = sb("srow3", [1, 3], F32)
    r11 = sb("r11", [1, 1], F32)
    negmh = sb("negmh", [128, 1], F32)
    zrow = sb("zrow", [1, 128], F32)
    ps_big = ps("ps_big", [128, 2048], F32)
    ps_a = ps("ps_a", [128, 512], F32)
    ps_b = ps("ps_b", [128, 512], F32)
    ps_c = ps("ps_c", [128, 512], F32)
    ps_d = ps("ps_d", [128, 512], F32)

    @blk.sync
    def _(e):
        ctx.inc(e.dma_start(out=srow3[0:1, :], in_=io["s012"].unsqueeze(0)), "wdma", 16)

    @blk.vector
    def _(e):
        e.memset(ones_col[:], 1.0)
        e.memset(ones_colh[:], 1.0)
        e.memset(ones_colb[:], 1.0)
        ctx.inc(e.memset(ones_row[:], 1.0), "dve")

    @blk.gpsimd
    def _(e):
        ctx.fence(e, e.memset(identb[:], 0.0), "gps")
        ctx.fence(e, e.affine_select(out=identb[:], in_=identb[:], compare_op=ALU.not_equal,
                        fill=1.0, base=0, pattern=[[-1, 128]], channel_multiplier=1), "gps")
        ctx.fence(e, e.memset(tril32[:], 1.0), "gps")
        ctx.fence(e, e.affine_select(out=tril32[:], in_=tril32[:], compare_op=ALU.is_ge,
                        fill=0.0, base=0, pattern=[[1, 128]], channel_multiplier=-1), "gps")
        ctx.inc(e.engine_nop(), "gps")

    @blk.vector
    def _(e):
        ctx.wait(e, "gps", ctx.c["gps"])
        ctx.inc(e.tensor_copy(trilf[:], tril32[:]), "dve")

    @blk.tensor
    def _(e):
        ctx.wait(e, "wdma", ctx.c["wdma"])
        ctx.wait(e, "dve", ctx.c["dve"])
        ctx.inc(e.matmul(ps_d[:, 0:3], ones_row[:], srow3[:], start=True, stop=True), "pe")

    @blk.scalar
    def _(e):
        ctx.wait(e, "pe", ctx.c["pe"])
        ctx.inc(e.activation(scols2[:], ps_d[:, 0:3], AF.Copy, scale=2.0), "act")
    ctx.barrier()

    with ExitStack() as ph:
        xch = [ctx.sb(f"xch{i}", [128, T], F32, ph) for i in range(2)]
        sqch = [ctx.sb(f"sqch{i}", [128, T], F32, ph) for i in range(2)]
        xnf = [ctx.sb(f"xnf{i}", [128, TTW], F32, ph) for i in range(2)]
        xt = [ctx.sb(f"xt{i}", [128, TTW], F32, ph) for i in range(2)]
        stats_r = ctx.sb("stats_r", [1, T], F32, ph)
        stats_v = ctx.sb("stats_v", [1, T], F32, ph)
        stats_m = ctx.sb("stats_m", [1, T], F32, ph)

        pe_ln = {}
        act_sq = {}
        for c in range(DC):
            slot = "xda" if c % 2 == 0 else "xdb"
            @blk.sync
            def _(e, c=c, slot=slot):
                if c >= 2:
                    ctx.wait(e, "pe", pe_ln[c - 2])
                ctx.inc(e.dma_start(out=xch[c % 2][:], in_=io["x"][c * 128:(c + 1) * 128, :]), slot, 16)
            x_at = ctx.c[slot]
            @blk.scalar
            def _(e, c=c, slot=slot, x_at=x_at):
                ctx.wait(e, slot, x_at)
                if c >= 2:
                    ctx.wait(e, "pe", pe_ln[c - 2])
                act_sq[c] = ctx.inc(e.activation(sqch[c % 2][:], xch[c % 2][:], AF.Square), "act")
            @blk.tensor
            def _(e, c=c):
                ctx.wait(e, "act", act_sq[c])
                for tt in range(NTT):
                    e.matmul(ps_big[0:1, tt * TTW:(tt + 1) * TTW], ones_col[:],
                             xch[c % 2][:, tt * TTW:(tt + 1) * TTW],
                             start=(c == 0), stop=(c == DC - 1))
                last = None
                for tt in range(NTT):
                    last = e.matmul(ps_big[32:33, tt * TTW:(tt + 1) * TTW], ones_col[:],
                                    sqch[c % 2][:, tt * TTW:(tt + 1) * TTW],
                                    start=(c == 0), stop=(c == DC - 1))
                pe_ln[c] = ctx.inc(last, "pe")

        @blk.vector
        def _(e):
            ctx.wait(e, "pe", pe_ln[DC - 1])
            ctx.fence(e, e.tensor_scalar(stats_r[:], ps_big[0:1, 0:T], 1.0 / D, None, ALU.mult))
            ctx.fence(e, e.tensor_scalar(stats_v[:], ps_big[32:33, 0:T], 1.0 / D, None, ALU.mult))
            ctx.fence(e, e.tensor_tensor(stats_m[:], stats_r[:], stats_r[:], ALU.mult))
            ctx.fence(e, e.tensor_sub(stats_v[:], stats_v[:], stats_m[:]))
            ctx.inc(e.tensor_scalar(stats_v[:], stats_v[:], 1e-5, None, ALU.add), "dve")
        @blk.scalar
        def _(e):
            ctx.wait(e, "dve", ctx.c["dve"])
            e.activation(stats_v[:], stats_v[:], AF.Ln)
            ctx.inc(e.activation(stats_v[:], stats_v[:], AF.Exp, scale=-0.5), "act")
        @blk.vector
        def _(e):
            ctx.wait(e, "act", ctx.c["act"])
            ctx.inc(e.tensor_tensor(stats_m[:], stats_r[:], stats_v[:], ALU.mult), "dve")
        stats_done = ctx.snap()

        dve_p2 = {}
        pe_bc = {}
        dma_p2 = {}
        for k in range(NTT * DC):
            tt, c = divmod(k, DC)
            if c == 0:
                @blk.tensor
                def _(e, tt=tt, k=k):
                    ctx.wait(e, stats_done)
                    if k >= 1:
                        ctx.wait(e, "dve", dve_p2[k - 1])
                    e.matmul(ps_a[:, 0:TTW], ones_row[:], stats_v[0:1, tt * TTW:(tt + 1) * TTW], start=True, stop=True)
                    i = e.matmul(ps_b[:, 0:TTW], ones_row[:], stats_m[0:1, tt * TTW:(tt + 1) * TTW], start=True, stop=True)
                    pe_bc[tt] = ctx.inc(i, "pe")
            slot2 = "xda" if k % 2 == 0 else "xdb"
            @blk.sync
            def _(e, k=k, tt=tt, c=c, slot2=slot2):
                ctx.wait(e, stats_done)
                if k >= 2:
                    ctx.wait(e, "dve", dve_p2[k - 2])
                ctx.inc(e.dma_start(
                    out=xt[k % 2][:],
                    in_=io["x"][c * 128:(c + 1) * 128, tt * TTW:(tt + 1) * TTW]), slot2, 16)
            dma_p2[k] = (slot2, ctx.c[slot2])
            @blk.vector
            def _(e, k=k, tt=tt, c=c):
                ctx.wait(e, dma_p2[k][0], dma_p2[k][1])
                ctx.wait(e, "pe", pe_bc[tt])
                e.tensor_tensor(xnf[k % 2][:], xt[k % 2][:], ps_a[:, 0:TTW], ALU.mult)
                dst = A_xn[:, c * T + tt * TTW: c * T + (tt + 1) * TTW]
                dve_p2[k] = ctx.inc(e.tensor_tensor(dst, xnf[k % 2][:], ps_b[:, 0:TTW], ALU.subtract), "dve")
        ctx.barrier()
    ctx.tap("xn1", A_xn[:], [128, DC * T], BF16)
    if upto <= 1:
        return

    with ExitStack() as ph:
        ustage = [ctx.sb(f"ust{i}", [128, 4096], U8, ph) for i in range(2)]
        ubot = [ctx.sb(f"ubt{i}", [128, 4096], U8, ph) for i in range(2)]
        wout = [ctx.sb(f"wcv{i}", [128, 4096], BF16, ph) for i in range(2)]

        cw = {-2: 0, -1: 0}
        stq = {-2: ("sta", 0), -1: ("stb", 0)}
        jobs = []
        for j in range(8):
            jobs.append(("d0", io["k0"][j * 128:(j + 1) * 128, :],
                         io["k0"][D + j * 128: D + (j + 1) * 128, :],
                         io["d0s"][j * 128:(j + 1) * 128, :], 4096))
        for j in range(32):
            jobs.append(("pm", io["k1"][j * 128:(j + 1) * 128, :], None,
                         io["w1s"][j * 128:(j + 1) * 128, :], 4096))
        for j in range(32):
            jobs.append(("pm", io["k2"][j * 128:(j + 1) * 128, :], None,
                         io["w2s"][j * 128:(j + 1) * 128, :], 1024))
        for j, (kind, top_ap, bot_ap, dst_ap, w) in enumerate(jobs):
            sl_t = "wda" if j % 2 == 0 else "wdb"
            sl_b = "xda" if j % 2 == 0 else "xdb"
            @blk.sync
            def _(e, j=j, kind=kind, top_ap=top_ap, bot_ap=bot_ap, w=w, sl_t=sl_t, sl_b=sl_b):
                ctx.wait(e, "dve", cw[j - 2])
                ctx.inc(e.dma_start(out=ustage[j % 2][:, 0:w], in_=top_ap), sl_t, 16)
                if kind == "d0":
                    ctx.inc(e.dma_start(out=ubot[j % 2][:, 0:w], in_=bot_ap), sl_b, 16)
            at_t = ctx.c[sl_t]
            at_b = ctx.c[sl_b] if kind == "d0" else 0
            @blk.vector
            def _(e, j=j, kind=kind, w=w, sl_t=sl_t, sl_b=sl_b, at_t=at_t, at_b=at_b):
                ctx.wait(e, sl_t, at_t)
                if kind == "d0":
                    ctx.wait(e, sl_b, at_b)
                ctx.wait(e, stq[j - 2][0], stq[j - 2][1])
                if kind == "d0":
                    ctx.inc(e.tensor_tensor(wout[j % 2][:, 0:w], ustage[j % 2][:, 0:w],
                                            ubot[j % 2][:, 0:w], ALU.subtract), "dve")
                else:
                    ctx.inc(e.tensor_scalar(wout[j % 2][:, 0:w], ustage[j % 2][:, 0:w],
                                            2.0, -1.0, ALU.mult, ALU.add), "dve")
            cw[j] = ctx.c["dve"]
            sl_s = "sta" if j % 2 == 0 else "stb"
            @blk.sync
            def _(e, j=j, dst_ap=dst_ap, w=w, sl_s=sl_s):
                ctx.wait(e, "dve", cw[j])
                ctx.inc(e.dma_start(out=dst_ap, in_=wout[j % 2][:, 0:w]), sl_s, 16)
            stq[j] = (sl_s, ctx.c[sl_s])
        ctx.barrier()
    if upto <= 2:
        return

    prep_ph = ExitStack()
    wstg = [ctx.sb(f"wstg{i}", [128, 1024], F32, prep_ph) for i in range(2)]
    wdkv_bf = ctx.sb("wdkv_bf", [128, 1024], BF16, prep_ph)
    wdq_bf = ctx.sb("wdq_bf", [128, 1024], BF16, prep_ph)
    wuq_bf = ctx.sb("wuq_bf", [128, 1024], BF16, prep_ph)
    wuqr_bf = ctx.sb("wuqr_bf", [128, 1024], BF16, prep_ph)
    wukr_bf = ctx.sb("wukr_bf", [128, 1024], BF16, prep_ph)
    wdqr_bf = ctx.sb("wdqr_bf", [128, 1024], BF16, prep_ph)
    wuvr_bf = ctx.sb("wuvr_bf", [128, 1024], BF16, prep_ph)
    wo_bf = ctx.sb("wo_bf", [128, 8 * 1024], BF16, prep_ph)
    inner_bf = ctx.sb("inner_bf", [128, 128], BF16, prep_ph)
    qlow_fm = ctx.sb("qlow_fm", [128, T], BF16, prep_ph)

    ldjobs = []
    for c in range(8):
        ldjobs.append((io["wdkv"][c * 128:(c + 1) * 128, :], wdkv_bf[:, c * 128:(c + 1) * 128], 128))
    for c in range(8):
        ldjobs.append((io["wdqf"][c * 128:(c + 1) * 128, :], wdq_bf[:, c * 128:(c + 1) * 128], 128))
    ldjobs.append((io["wuqf"], wuq_bf[:], 1024))
    for c in range(8):
        ldjobs.append((io["wuqr"][c * 128:(c + 1) * 128, :], wuqr_bf[:, c * 128:(c + 1) * 128], 128))
    for c in range(8):
        ldjobs.append((io["wuk"][c * 128:(c + 1) * 128, :], wukr_bf[:, c * 128:(c + 1) * 128], 128))
    ldjobs.append((io["wdqr"], wdqr_bf[:], 1024))
    for c in range(8):
        ldjobs.append((io["wuv"][c * 128:(c + 1) * 128, :], wuvr_bf[:, c * 128:(c + 1) * 128], 128))
    for c in range(8):
        ldjobs.append((io["wo"][c * 128:(c + 1) * 128, :], wo_bf[:, c * 1024:(c + 1) * 1024], 1024))
    lcv = {-2: 0, -1: 0}
    for j, (src, dst, w) in enumerate(ldjobs):
        slot = "wda" if j % 2 == 0 else "wdb"
        @blk.sync
        def _(e, j=j, src=src, w=w, slot=slot):
            ctx.wait(e, "dve", lcv[j - 2])
            ctx.inc(e.dma_start(out=wstg[j % 2][:, 0:w], in_=src), slot, 16)
        w_at = ctx.c[slot]
        @blk.vector
        def _(e, j=j, dst=dst, w=w, slot=slot, w_at=w_at):
            ctx.wait(e, slot, w_at)
            ctx.inc(e.tensor_copy(dst, wstg[j % 2][:, 0:w]), "dve")
        lcv[j] = ctx.c["dve"]
    wconv = ctx.c["dve"]

    evac_at = {-2: 0, -1: 0}
    for tt in range(NTT):
        pp = ps_a if tt % 2 == 0 else ps_b
        @blk.tensor
        def _(e, tt=tt, pp=pp):
            ctx.wait(e, "dve", wconv)
            ctx.wait(e, "act", evac_at[tt - 2])
            last = None
            for ci in range(DC):
                last = e.matmul(pp[:, 0:TTW], wdkv_bf[:, ci * 128:(ci + 1) * 128],
                                A_xn[:, ci * T + tt * TTW: ci * T + (tt + 1) * TTW],
                                start=(ci == 0), stop=(ci == DC - 1))
            ctx.inc(last, "pe")
        p_at = ctx.c["pe"]
        @blk.scalar
        def _(e, tt=tt, pp=pp, p_at=p_at):
            ctx.wait(e, "pe", p_at)
            ctx.inc(e.activation(ckv_fm[:, tt * TTW:(tt + 1) * TTW], pp[:, 0:TTW], AF.Copy), "act")
        evac_at[tt] = ctx.c["act"]

    for tt in range(NTT):
        pp = ps_a if tt % 2 == 0 else ps_b
        @blk.tensor
        def _(e, tt=tt, pp=pp):
            ctx.wait(e, "act", evac_at[tt + 2])
            last = None
            for ci in range(DC):
                last = e.matmul(pp[:, 0:TTW], wdq_bf[:, ci * 128:(ci + 1) * 128],
                                A_xn[:, ci * T + tt * TTW: ci * T + (tt + 1) * TTW],
                                start=(ci == 0), stop=(ci == DC - 1))
            ctx.inc(last, "pe")
        p_at = ctx.c["pe"]
        @blk.scalar
        def _(e, tt=tt, pp=pp, p_at=p_at):
            ctx.wait(e, "pe", p_at)
            ctx.inc(e.activation(qlow_fm[:, tt * TTW:(tt + 1) * TTW], pp[:, 0:TTW], AF.Copy), "act")
        evac_at[tt + 4] = ctx.c["act"]

    k = 8
    for qc in range(8):
        for tt in range(NTT):
            pp = ps_a if k % 2 == 0 else ps_b
            @blk.tensor
            def _(e, qc=qc, tt=tt, pp=pp, k=k):
                ctx.wait(e, "act", evac_at[k - 2])
                ctx.inc(e.matmul(pp[:, 0:TTW], wuq_bf[:, qc * 128:(qc + 1) * 128],
                                 qlow_fm[:, tt * TTW:(tt + 1) * TTW], start=True, stop=True), "pe")
            p_at = ctx.c["pe"]
            @blk.scalar
            def _(e, qc=qc, tt=tt, pp=pp, p_at=p_at):
                ctx.wait(e, "pe", p_at)
                ctx.inc(e.activation(q1_fm[:, qc * T + tt * TTW: qc * T + (tt + 1) * TTW],
                                     pp[:, 0:TTW], AF.Copy), "act")
            evac_at[k] = ctx.c["act"]
            k += 1

    for kc in range(KC):
        pp = ps_a if kc % 2 == 0 else ps_b
        @blk.tensor
        def _(e, kc=kc, pp=pp, k=k):
            ctx.wait(e, "act", evac_at[k - 2])
            ctx.inc(e.transpose(pp[:, 0:64].bitcast(BF16), ckv_fm[:, kc * 128:(kc + 1) * 128],
                                identb[:]), "pe")
        p_at = ctx.c["pe"]
        @blk.scalar
        def _(e, kc=kc, pp=pp, p_at=p_at):
            ctx.wait(e, "pe", p_at)
            ctx.inc(e.activation(ckv_tok[:, kc * 128:(kc + 1) * 128], pp[:, 0:64].bitcast(BF16),
                                 AF.Copy), "act")
        evac_at[k] = ctx.c["act"]
        k += 1

    @blk.tensor
    def _(e, k=k):
        ctx.wait(e, "act", evac_at[k - 1])
        last = None
        for kc in range(KC):
            last = e.matmul(ps_c[:, 0:1], ckv_tok[:, kc * 128:(kc + 1) * 128], ones_colh[:],
                            start=(kc == 0), stop=(kc == KC - 1))
        for kc in range(KC):
            last = e.matmul(ps_d[:, 0:128], ckv_tok[:, kc * 128:(kc + 1) * 128],
                            ckv_tok[:, kc * 128:(kc + 1) * 128],
                            start=(kc == 0), stop=(kc == KC - 1))
        ctx.inc(last, "pe")
    pg_at = ctx.c["pe"]
    @blk.scalar
    def _(e, pg_at=pg_at):
        ctx.wait(e, "pe", pg_at)
        e.activation(ckvbar_col[:], ps_c[:, 0:1], AF.Copy, scale=1.0 / T)
        ctx.inc(e.activation(G1_sb[:], ps_d[:, 0:128], AF.Copy, scale=1.0 / T), "act")
    gbar_at = ctx.c["act"]

    @blk.tensor
    def _(e, gbar_at=gbar_at):
        ctx.wait(e, "act", gbar_at)
        last = None
        for ci in range(DC):
            last = e.matmul(ps_a[:, 0:128], wuqr_bf[:, ci * 128:(ci + 1) * 128],
                            wukr_bf[:, ci * 128:(ci + 1) * 128],
                            start=(ci == 0), stop=(ci == DC - 1))
        ctx.inc(last, "pe")
    pi_at = ctx.c["pe"]
    @blk.scalar
    def _(e, pi_at=pi_at):
        ctx.wait(e, "pe", pi_at)
        ctx.inc(e.activation(inner_bf[:], ps_a[:, 0:128], AF.Copy), "act")
    ai_at = ctx.c["act"]
    for c in range(8):
        pp = ps_a if c % 2 == 0 else ps_b
        @blk.tensor
        def _(e, c=c, pp=pp, ai_at=ai_at):
            ctx.wait(e, "act", evac_at.get(k + c - 2, ai_at) if c >= 2 else ai_at)
            if c >= 2:
                ctx.wait(e, "act", evac_at[k + c - 2])
            ctx.inc(e.matmul(pp[:, 0:128], wdqr_bf[:, c * 128:(c + 1) * 128], inner_bf[:],
                             start=True, stop=True), "pe")
        p_at = ctx.c["pe"]
        @blk.scalar
        def _(e, c=c, pp=pp, p_at=p_at):
            ctx.wait(e, "pe", p_at)
            ctx.inc(e.activation(keff_sb[:, c * 128:(c + 1) * 128], pp[:, 0:128], AF.Copy), "act")
        evac_at[k + c] = ctx.c["act"]
    k += 8

    for half in range(2):
        pp = ps_a if half % 2 == 0 else ps_b
        @blk.tensor
        def _(e, half=half, pp=pp, k=k):
            ctx.wait(e, "act", evac_at[k + half - 2])
            last = None
            for ci in range(DC):
                last = e.matmul(pp[:, 0:TTW], wuvr_bf[:, ci * 128:(ci + 1) * 128],
                                wo_bf[:, ci * 1024 + half * 512: ci * 1024 + half * 512 + 512],
                                start=(ci == 0), stop=(ci == DC - 1))
            ctx.inc(last, "pe")
        p_at = ctx.c["pe"]
        @blk.scalar
        def _(e, half=half, pp=pp, p_at=p_at):
            ctx.wait(e, "pe", p_at)
            ctx.inc(e.activation(veff_sb[:, half * 512:(half + 1) * 512], pp[:, 0:TTW], AF.Copy), "act")
        evac_at[k + half] = ctx.c["act"]
    ctx.barrier()
    prep_ph.close()
    ctx.tap("ckv", ckv_fm[:], [128, T], BF16)
    ctx.tap("q1", q1_fm[:], [128, DC * T], BF16)
    ctx.tap("keff", keff_sb[:], [128, 8 * 128], BF16)
    ctx.tap("veff", veff_sb[:], [128, 1024], BF16)
    ctx.tap("G1", G1_sb[:], [128, 128], BF16)
    ctx.tap("ckvtok", ckv_tok[:], [128, KC * 128], FP16)
    if upto <= 3:
        return

    att_ph = ExitStack()
    qlh_fm = ctx.sb("qlh_fm", [128, T], BF16, att_ph)
    p_sb = ctx.sb("p_sb", [128, T], FP16, att_ph)
    prod_sb = ctx.sb("prod_sb", [128, T], BF16, att_ph)
    ctx_sb = ctx.sb("ctx_sb", [128, 128], BF16, att_ph)
    ctxn_sb = ctx.sb("ctxn_sb", [128, 128], BF16, att_ph)
    mu_row = ctx.sb("mu_row", [1, T], F32, att_ph)
    ex2_row = ctx.sb("ex2_row", [1, T], F32, att_ph)
    sig_row = ctx.sb("sig_row", [1, T], F32, att_ph)
    scr_row = ctx.sb("scr_row", [1, T], F32, att_ph)
    emh_row = ctx.sb("emh_row", [1, T], F32, att_ph)

    prev = {"pe_sc": 0, "a_exp": 0, "v_mask": 0, "v_norm": 0, "p_av": 0, "a_ctx": 0,
            "p_vp": 0, "a_y": 0}
    for h in range(H):
        qb = (h % 2) * 64
        hp = h // 2
        ev_q = {}
        for tt in range(NTT):
            pp = ps_a if tt % 2 == 0 else ps_b
            @blk.tensor
            def _(e, tt=tt, pp=pp):
                if tt < 2:
                    ctx.wait(e, "dve", prev["v_norm"])
                    ctx.wait(e, "act", max(prev["a_ctx"], prev["a_y"]))
                ctx.wait(e, "act", ev_q.get(tt - 2, 0))
                ctx.inc(e.matmul(pp[:, 0:TTW], keff_sb[qb:qb + 64, hp * 128:(hp + 1) * 128],
                                 q1_fm[qb:qb + 64, hp * T + tt * TTW: hp * T + (tt + 1) * TTW],
                                 start=True, stop=True), "pe")
            p_at = ctx.c["pe"]
            @blk.scalar
            def _(e, tt=tt, pp=pp, p_at=p_at, h=h):
                ctx.wait(e, "pe", p_at)
                if tt == 0:
                    ctx.wait(e, "pe", prev["pe_sc"])
                ctx.inc(e.activation(qlh_fm[:, tt * TTW:(tt + 1) * TTW], pp[:, 0:TTW], AF.Copy), "act")
            ev_q[tt] = ctx.c["act"]

        @blk.tensor
        def _(e, h=h):
            ctx.wait(e, "act", ev_q[NTT - 1])
            ctx.wait(e, "dve", prev["v_mask"])
            ctx.wait(e, "act", prev["a_exp"])
            last = None
            for tt in range(NTT):
                last = e.matmul(ps_big[:, tt * TTW:(tt + 1) * TTW], G1_sb[:],
                                qlh_fm[:, tt * TTW:(tt + 1) * TTW], start=True, stop=True)
            ctx.inc(last, "pe")
        pgq = ctx.c["pe"]
        for tt in range(NTT):
            pp = ps_c if tt % 2 == 0 else ps_d
            @blk.tensor
            def _(e, tt=tt, pp=pp, h=h):
                if tt < 2:
                    ctx.wait(e, "dve", prev["v_norm"])
                    ctx.wait(e, "act", max(prev["a_ctx"], prev["a_y"]))
                ctx.wait(e, "act", ev_q.get(10 + tt - 2, 0))
                ctx.inc(e.matmul(pp[0:1, 0:TTW], ckvbar_col[:],
                                 qlh_fm[:, tt * TTW:(tt + 1) * TTW], start=True, stop=True), "pe")
            p_at = ctx.c["pe"]
            @blk.scalar
            def _(e, tt=tt, pp=pp, p_at=p_at):
                ctx.wait(e, "pe", p_at)
                ctx.inc(e.activation(mu_row[0:1, tt * TTW:(tt + 1) * TTW], pp[0:1, 0:TTW], AF.Copy), "act")
            ev_q[10 + tt] = ctx.c["act"]
        @blk.vector
        def _(e, pgq=pgq):
            ctx.wait(e, "pe", pgq)
            ctx.inc(e.tensor_tensor(prod_sb[:], ps_big[:, 0:T], qlh_fm[:], ALU.mult), "dve")
        vprod = ctx.c["dve"]
        for tt in range(NTT):
            pp = ps_c if tt % 2 == 0 else ps_d
            @blk.tensor
            def _(e, tt=tt, pp=pp, vprod=vprod):
                ctx.wait(e, "dve", vprod)
                ctx.wait(e, "act", ev_q[13] if tt < 2 else ev_q.get(20 + tt - 2, 0))
                ctx.inc(e.matmul(pp[0:1, 0:TTW], ones_colb[:],
                                 prod_sb[:, tt * TTW:(tt + 1) * TTW], start=True, stop=True), "pe")
            p_at = ctx.c["pe"]
            @blk.scalar
            def _(e, tt=tt, pp=pp, p_at=p_at):
                ctx.wait(e, "pe", p_at)
                ctx.inc(e.activation(ex2_row[0:1, tt * TTW:(tt + 1) * TTW], pp[0:1, 0:TTW], AF.Copy), "act")
            ev_q[20 + tt] = ctx.c["act"]

        @blk.vector
        def _(e):
            ctx.wait(e, "act", ev_q[23])
            e.tensor_tensor(scr_row[:], mu_row[:], mu_row[:], ALU.mult)
            e.tensor_tensor(sig_row[:], ex2_row[:], scr_row[:], ALU.subtract)
            ctx.inc(e.tensor_scalar(sig_row[:], sig_row[:], 1e-12, None, ALU.max), "dve")
        vrow = ctx.c["dve"]
        @blk.scalar
        def _(e, vrow=vrow):
            ctx.wait(e, "dve", vrow)
            e.activation(sig_row[:], sig_row[:], AF.Ln)
            ctx.inc(e.activation(sig_row[:], sig_row[:], AF.Exp, scale=0.5), "act")
        arow = ctx.c["act"]
        @blk.vector
        def _(e, arow=arow):
            ctx.wait(e, "act", arow)
            ctx.inc(e.scalar_tensor_tensor(scr_row[:], sig_row[:], 6.0, mu_row[:], ALU.mult, ALU.add), "dve")
        vrow2 = ctx.c["dve"]
        @blk.scalar
        def _(e, vrow2=vrow2):
            ctx.wait(e, "dve", vrow2)
            ctx.inc(e.activation(emh_row[:], scr_row[:], AF.Exp, scale=0.125), "act")
        aemh = ctx.c["act"]

        for qt in range(QT):
            nkc = qt + 1
            @blk.vector
            def _(e, qt=qt, aemh=aemh):
                ctx.wait(e, "act", aemh)
                ctx.inc(e.reduce_sum(r11[:], emh_row[0:1, qt * 128:(qt + 1) * 128], axis=AX.X), "dve")
            vmh = ctx.c["dve"]
            @blk.scalar
            def _(e, vmh=vmh):
                ctx.wait(e, "dve", vmh)
                ctx.inc(e.activation(r11[:], r11[:], AF.Ln), "act")
            amh = ctx.c["act"]
            @blk.tensor
            def _(e, amh=amh):
                ctx.wait(e, "act", amh)
                ctx.wait(e, "act", prev["a_ctx"])
                ctx.inc(e.matmul(ps_d[:, 300:301], ones_row[:], r11[:], start=True, stop=True), "pe")
            pmh = ctx.c["pe"]
            @blk.scalar
            def _(e, pmh=pmh):
                ctx.wait(e, "pe", pmh)
                ctx.inc(e.activation(negmh[:], ps_d[:, 300:301], AF.Copy, scale=-1.0), "act")
            anm = ctx.c["act"]
            @blk.tensor
            def _(e, qt=qt, nkc=nkc):
                ctx.wait(e, "dve", vprod if qt == 0 else 0)
                ctx.wait(e, "act", prev["a_exp"])
                last = None
                for kc in range(nkc):
                    last = e.matmul(ps_big[:, kc * 128:(kc + 1) * 128],
                                    ckv_fm[:, kc * 128:(kc + 1) * 128],
                                    qlh_fm[:, qt * 128:(qt + 1) * 128], start=True, stop=True)
                ctx.inc(last, "pe")
            psc = ctx.c["pe"]
            prev["pe_sc"] = psc
            @blk.scalar
            def _(e, qt=qt, nkc=nkc, psc=psc, anm=anm):
                ctx.wait(e, "pe", psc)
                ctx.wait(e, "act", anm)
                ctx.wait(e, "pe", prev["p_av"])
                for kc in range(nkc):
                    e.activation(p_sb[:, kc * 128:(kc + 1) * 128], ps_big[:, kc * 128:(kc + 1) * 128],
                                 AF.Exp, scale=0.125, bias=negmh[:])
                ctx.inc(e.nop(), "act")
            aexp = ctx.c["act"]
            prev["a_exp"] = aexp
            @blk.vector
            def _(e, qt=qt, aexp=aexp):
                ctx.wait(e, "act", aexp)
                ctx.inc(e.tensor_tensor(p_sb[:, qt * 128:(qt + 1) * 128],
                                        p_sb[:, qt * 128:(qt + 1) * 128], trilf[:], ALU.mult), "dve")
            vmask = ctx.c["dve"]
            prev["v_mask"] = vmask
            @blk.tensor
            def _(e, qt=qt, nkc=nkc, vmask=vmask):
                ctx.wait(e, "dve", vmask)
                ctx.wait(e, "act", prev["a_ctx"])
                last = None
                for kc in range(nkc):
                    e.matmul(ps_c[0:128, 0:128], ckv_tok[:, kc * 128:(kc + 1) * 128],
                             p_sb[:, kc * 128:(kc + 1) * 128],
                             start=(kc == 0), stop=(kc == nkc - 1))
                    last = e.matmul(ps_d[0:1, 0:128], ones_colh[:],
                                    p_sb[:, kc * 128:(kc + 1) * 128],
                                    start=(kc == 0), stop=(kc == nkc - 1))
                ctx.inc(last, "pe")
            pav = ctx.c["pe"]
            prev["p_av"] = pav
            @blk.scalar
            def _(e, pav=pav):
                ctx.wait(e, "pe", pav)
                ctx.wait(e, "dve", prev["v_norm"])
                e.activation(zrow[:], ps_d[0:1, 0:128], AF.Ln)
                e.activation(zrow[:], zrow[:], AF.Exp, scale=-1.0)
                ctx.inc(e.copy(ctx_sb[:], ps_c[0:128, 0:128]), "act")
            actx = ctx.c["act"]
            prev["a_ctx"] = actx
            @blk.tensor
            def _(e, actx=actx):
                ctx.wait(e, "act", actx)
                ctx.inc(e.matmul(ps_d[0:128, 128:256], ones_row[:], zrow[:],
                                 start=True, stop=True), "pe")
            przb = ctx.c["pe"]
            @blk.vector
            def _(e, przb=przb):
                ctx.wait(e, "pe", przb)
                ctx.wait(e, "pe", prev["p_vp"])
                ctx.inc(e.tensor_tensor(ctxn_sb[:], ctx_sb[:], ps_d[0:128, 128:256], ALU.mult), "dve")
            vnorm = ctx.c["dve"]
            prev["v_norm"] = vnorm
            @blk.tensor
            def _(e, h=h, vnorm=vnorm):
                ctx.wait(e, "dve", vnorm)
                ctx.wait(e, "act", prev["a_y"])
                ctx.inc(e.matmul(ps_b[0:64, 0:128], veff_sb[:, h * 64:(h + 1) * 64], ctxn_sb[:],
                                 start=True, stop=True), "pe")
            pvp = ctx.c["pe"]
            prev["p_vp"] = pvp
            @blk.scalar
            def _(e, h=h, qt=qt, pvp=pvp, qb=qb, hp=hp):
                ctx.wait(e, "pe", pvp)
                ydst = A_y[qb:qb + 64, hp * T + qt * 128: hp * T + (qt + 1) * 128]
                ctx.inc(e.activation(ydst, ps_b[0:64, 0:128], AF.Copy), "act")
            prev["a_y"] = ctx.c["act"]
    ctx.barrier()
    att_ph.close()
    ctx.tap("y1", A_y[:], [128, DC * T], BF16)
    if upto <= 6:
        return

    ffn_ph = ExitStack()
    wc1 = [ctx.sb(f"wc1_{i}", [128, 1024], BF16, ffn_ph) for i in range(2)]
    wc2 = [ctx.sb(f"wc2_{i}", [128, 4096], BF16, ffn_ph) for i in range(2)]
    a1_sb = ctx.sb("a1_sb", [128, 32 * TTW], BF16, ffn_ph)
    a2_sb = ctx.sb("a2_sb", [128, 32 * TTW], BF16, ffn_ph)
    g1co = [ctx.sb(f"g1co{i}", [128, TTW], BF16, ffn_ph) for i in range(2)]
    yg = ctx.sb("yg", [128, TTW], F32, ffn_ph)
    xt2 = [ctx.sb(f"xt2_{i}", [128, TTW], F32, ffn_ph) for i in range(2)]
    out_sb = [ctx.sb(f"outsb{i}", [128, TTW], F32, ffn_ph) for i in range(2)]

    import os as _os
    _ffn_ntt = int(_os.environ.get("FFN_NTT", str(NTT)))
    _ffn_l = _os.environ.get("FFN_L", "123")
    wpe = {}
    evq = {-2: 0, -1: 0}
    gdve = {0: 0, 1: 0}
    xdve = {0: 0, 1: 0}
    ost = {0: ("sta", 0), 1: ("stb", 0)}
    g = 0
    for tt in range(_ffn_ntt):
        for co in range(32 if "1" in _ffn_l else 0):
            slot = "wda" if g % 2 == 0 else "wdb"
            @blk.sync
            def _(e, co=co, slot=slot, g=g, s_at=ctx.c.raw(slot)):
                ctx.wait(e, slot, s_at)
                ctx.wait(e, "pe", wpe.get(("wc1", g % 2), 0))
                ctx.inc(e.dma_start(
                    out=wc1[g % 2][:].rearrange("p (ci c) -> p ci c", ci=8),
                    in_=io["d0s"][:, co * 128:(co + 1) * 128].rearrange(
                        "(ci p) c -> p ci c", ci=8)), slot, 16)
            w_at = ctx.c[slot]
            pp = ps_a if g % 2 == 0 else ps_b
            @blk.tensor
            def _(e, co=co, tt=tt, pp=pp, slot=slot, w_at=w_at, g=g):
                ctx.wait(e, slot, w_at)
                ctx.wait(e, "act", evq[g - 2])
                last = None
                for ci in range(DC):
                    last = e.matmul(pp[:, 0:TTW], wc1[g % 2][:, ci * 128:(ci + 1) * 128],
                                    A_xn[:, ci * T + tt * TTW: ci * T + (tt + 1) * TTW],
                                    start=(ci == 0), stop=(ci == DC - 1))
                ctx.inc(last, "pe")
            wpe[("wc1", g % 2)] = ctx.c["pe"]
            p_at = ctx.c["pe"]
            @blk.scalar
            def _(e, co=co, pp=pp, p_at=p_at):
                ctx.wait(e, "pe", p_at)
                ctx.inc(e.activation(a1_sb[:, co * TTW:(co + 1) * TTW], pp[:, 0:TTW],
                                     AF.Relu, scale=scols2[:, 0:1]), "act")
            evq[g] = ctx.c["act"]
            g += 1
        l1_done = evq[g - 1]
        for co in range(32 if "2" in _ffn_l else 0):
            slot = "wda" if g % 2 == 0 else "wdb"
            @blk.sync
            def _(e, co=co, slot=slot, g=g, s_at=ctx.c.raw(slot)):
                ctx.wait(e, slot, s_at)
                ctx.wait(e, "pe", wpe.get(("wc2", g % 2), 0))
                ctx.inc(e.dma_start(
                    out=wc2[g % 2][:].rearrange("p (ci c) -> p ci c", ci=32),
                    in_=io["w1s"][:, co * 128:(co + 1) * 128].rearrange(
                        "(ci p) c -> p ci c", ci=32)), slot, 16)
            w_at = ctx.c[slot]
            pp = ps_a if g % 2 == 0 else ps_b
            @blk.tensor
            def _(e, co=co, pp=pp, slot=slot, w_at=w_at, g=g, l1_done=l1_done):
                ctx.wait(e, slot, w_at)
                ctx.wait(e, "act", l1_done if co == 0 else 0)
                ctx.wait(e, "act", evq[g - 2])
                last = None
                for ci in range(32):
                    last = e.matmul(pp[:, 0:TTW], wc2[g % 2][:, ci * 128:(ci + 1) * 128],
                                    a1_sb[:, ci * TTW:(ci + 1) * TTW],
                                    start=(ci == 0), stop=(ci == 31))
                ctx.inc(last, "pe")
            wpe[("wc2", g % 2)] = ctx.c["pe"]
            p_at = ctx.c["pe"]
            @blk.scalar
            def _(e, co=co, pp=pp, p_at=p_at):
                ctx.wait(e, "pe", p_at)
                ctx.inc(e.activation(a2_sb[:, co * TTW:(co + 1) * TTW], pp[:, 0:TTW],
                                     AF.Relu, scale=scols2[:, 1:2]), "act")
            evq[g] = ctx.c["act"]
            g += 1
        l2_done = evq[g - 1]
        for co in range(8 if "3" in _ffn_l else 0):
            slot = "wda" if g % 2 == 0 else "wdb"
            @blk.sync
            def _(e, co=co, slot=slot, g=g, s_at=ctx.c.raw(slot)):
                ctx.wait(e, slot, s_at)
                ctx.wait(e, "pe", wpe.get(("wc2", g % 2), 0))
                ctx.inc(e.dma_start(
                    out=wc2[g % 2][:].rearrange("p (ci c) -> p ci c", ci=32),
                    in_=io["w2s"][:, co * 128:(co + 1) * 128].rearrange(
                        "(ci p) c -> p ci c", ci=32)), slot, 16)
            w_at = ctx.c[slot]
            slotx = "xda" if g % 2 == 0 else "xdb"
            @blk.sync
            def _(e, co=co, tt=tt, slotx=slotx, g=g):
                ctx.wait(e, "dve", xdve[g % 2])
                ctx.inc(e.dma_start(out=xt2[g % 2][:],
                                    in_=io["x"][co * 128:(co + 1) * 128, tt * TTW:(tt + 1) * TTW]), slotx, 16)
            x_at = ctx.c[slotx]
            pp = ps_a if g % 2 == 0 else ps_b
            @blk.tensor
            def _(e, co=co, pp=pp, slot=slot, w_at=w_at, g=g, l2_done=l2_done):
                ctx.wait(e, slot, w_at)
                ctx.wait(e, "act", l2_done if co == 0 else 0)
                ctx.wait(e, "act", evq[g - 2])
                last = None
                for ci in range(32):
                    last = e.matmul(pp[:, 0:TTW], wc2[g % 2][:, ci * 128:(ci + 1) * 128],
                                    a2_sb[:, ci * TTW:(ci + 1) * TTW],
                                    start=(ci == 0), stop=(ci == 31))
                ctx.inc(last, "pe")
            wpe[("wc2", g % 2)] = ctx.c["pe"]
            p_at = ctx.c["pe"]
            @blk.scalar
            def _(e, co=co, pp=pp, p_at=p_at, g=g):
                ctx.wait(e, "pe", p_at)
                ctx.wait(e, "dve", gdve[g % 2])
                ctx.inc(e.activation(g1co[g % 2][:], pp[:, 0:TTW], AF.Silu,
                                     scale=scols2[:, 2:3]), "act")
            ev3 = ctx.c["act"]
            evq[g] = ev3
            @blk.vector
            def _(e, co=co, tt=tt, slotx=slotx, x_at=x_at, g=g, ev3=ev3):
                ctx.wait(e, "act", ev3)
                ctx.wait(e, slotx, x_at)
                ctx.wait(e, ost[g % 2][0], ost[g % 2][1])
                e.tensor_tensor(yg[:], A_y[:, co * T + tt * TTW: co * T + (tt + 1) * TTW],
                                g1co[g % 2][:], ALU.mult)
                ctx.inc(e.tensor_tensor(out_sb[g % 2][:], yg[:], xt2[g % 2][:], ALU.add), "dve")
            gdve[g % 2] = ctx.c["dve"]
            xdve[g % 2] = ctx.c["dve"]
            v_at = ctx.c["dve"]
            slots = "sta" if g % 2 == 0 else "stb"
            @blk.sync
            def _(e, co=co, tt=tt, slots=slots, v_at=v_at, g=g):
                ctx.wait(e, "dve", v_at)
                ctx.inc(e.dma_start(out=io["out"][co * 128:(co + 1) * 128, tt * TTW:(tt + 1) * TTW],
                                    in_=out_sb[g % 2][:]), slots, 16)
            ost[g % 2] = (slots, ctx.c[slots])
            g += 1
    @blk.sync
    def _(e):
        e.wait_ge(ctx.sems["sta"], ctx.c["sta"])
        e.wait_ge(ctx.sems["stb"], ctx.c["stb"])
    ctx.barrier()
    ffn_ph.close()


def build_block1(dbg=False, upto=99):
    nc = bass.Bass("TRN2", target_bir_lowering=False)
    d = {}
    d["x"] = nc.dram_tensor("x1", [D, T], F32, kind="ExternalInput").ap()
    d["wdqf"] = nc.dram_tensor("wdq_fm", [D, 128], F32, kind="ExternalInput").ap()
    d["wuqf"] = nc.dram_tensor("wuq_fm", [128, D], F32, kind="ExternalInput").ap()
    d["wuqr"] = nc.dram_tensor("wuq_raw", [D, 128], F32, kind="ExternalInput").ap()
    d["wdqr"] = nc.dram_tensor("wdq_raw", [128, D], F32, kind="ExternalInput").ap()
    d["wdkv"] = nc.dram_tensor("wdkv_fm", [D, 128], F32, kind="ExternalInput").ap()
    d["wuk"] = nc.dram_tensor("wuk_raw", [D, 128], F32, kind="ExternalInput").ap()
    d["wuv"] = nc.dram_tensor("wuv_raw", [D, 128], F32, kind="ExternalInput").ap()
    d["wo"] = nc.dram_tensor("wo_fm", [D, D], F32, kind="ExternalInput").ap()
    d["k0"] = nc.dram_tensor("k0_u8", [2 * D, 4096], U8, kind="ExternalInput").ap()
    d["k1"] = nc.dram_tensor("k1_u8", [4096, 4096], U8, kind="ExternalInput").ap()
    d["k2"] = nc.dram_tensor("k2_u8", [4096, D], U8, kind="ExternalInput").ap()
    d["s012"] = nc.dram_tensor("s012", [3], F32, kind="ExternalInput").ap()
    d["out"] = nc.dram_tensor("out", [D, T], F32, kind="ExternalOutput").ap()
    scr = "ExternalOutput" if dbg else "Internal"
    d["d0s"] = nc.dram_tensor("d0_scr", [D, 4096], BF16, kind=scr).ap()
    d["w1s"] = nc.dram_tensor("w1_scr", [4096, 4096], BF16, kind=scr).ap()
    d["w2s"] = nc.dram_tensor("w2_scr", [4096, D], BF16, kind=scr).ap()
    outer = ExitStack()
    with nc.Block() as block:
        ctx = Ctx(nc, block, outer, dbg=dbg)
        emit_block1(ctx, d, upto=upto)
    outer.close()
    return nc



B0_DRAM_KEYS = ["qkv_w_fm", "proj_w_fm", "proj_b_cols", "feat_w_fm", "feat_b_cols",
                "ps1_w_fm", "ps1_b_cols", "ps2_w_fm", "ps2_b_cols", "ps3_w_fm",
                "ps3_b", "pattern_bias", "sparse_w", "gate_w_fm", "e_bias",
                "gu_fm", "dn_fm", "shg_fm", "shu_fm", "shd_fm"]
B1_DRAM_KEYS = ["wdq_fm", "wuq_fm", "wuq_raw", "wdq_raw", "wdkv_fm", "wuk_raw",
                "wuv_raw", "wo_fm", "k0_u8", "k1_u8", "k2_u8", "s012"]


def kernel(**inputs):
    A = host_prepare(inputs)
    x = np.asarray(inputs["L0_attn_x"], np.float32)
    n = x.shape[0]

    nc0 = build_block0()
    in_maps0 = []
    for b in range(n):
        m = {k: A[k] for k in B0_DRAM_KEYS}
        m["x"] = np.ascontiguousarray(x[b].T)
        in_maps0.append(m)
    res0 = run_bass_kernel_spmd(nc0, in_maps0, core_ids=list(range(n)))
    x1s = [np.asarray(res0.results[b]["x1"]) for b in range(n)]

    nc1 = build_block1()
    in_maps1 = []
    for b in range(n):
        m = {k: A[k] for k in B1_DRAM_KEYS}
        m["x1"] = x1s[b]
        in_maps1.append(m)
    res1 = run_bass_kernel_spmd(nc1, in_maps1, core_ids=list(range(n)))
    out = np.stack([np.asarray(res1.results[b]["out"]).T for b in range(n)])
    return np.ascontiguousarray(out.astype(np.float32))
